# version 01
# baseline (speedup 1.0000x reference)
import numpy as np
import concourse.bass as bass
import concourse.mybir as mybir
from concourse.bass_utils import run_bass_kernel_spmd

F32 = mybir.dt.float32
BF16 = mybir.dt.bfloat16
I32 = mybir.dt.int32
I8 = mybir.dt.int8
AF = mybir.ActivationFunctionType
ALU = mybir.AluOpType
AX = mybir.AxisListType.X

ENGINES = ("tensor", "vector", "scalar", "gpsimd", "sync")

D = 2048
N = 2048
NT = N // 128
HEADS = 16
CH = 128
NCH = N // CH
FI = 8192
FC = FI // 128
EPS_A = float(np.finfo(np.float32).eps)
SCALE_ATT = float(128.0) ** -0.5
LN_EPS = 1e-5
Q_EPS = 1e-5
C8192 = float(np.float32(1.0) / np.float32(np.sqrt(np.float32(8192.0))))


class Tok:
    __slots__ = ("res", "count", "sync")

    def __init__(self, res, count, sync=False):
        self.res = res
        self.count = count
        self.sync = sync


class Sched:
    def __init__(self, nc):
        self.nc = nc
        self.ops = {e: [] for e in ENGINES}
        self.counts = {e: 0 for e in ENGINES}
        self.watermark = {e: {} for e in ENGINES}
        self.channels = []
        self.chan_count = {}
        self.chan_engine = {}
        self._sems = {}
        self.gate = []

    def channel(self, name, engine="sync"):
        self.channels.append(name)
        self.chan_count[name] = 0
        self.chan_engine[name] = engine
        return name

    def set_gate(self, toks):
        self.gate = [t for t in toks if t is not None]

    def frontier(self):
        toks = [Tok(e, self.counts[e]) for e in ENGINES if self.counts[e] > 0]
        toks += [Tok(cn, self.chan_count[cn]) for cn in self.channels
                 if self.chan_count[cn] > 0]
        return toks

    def _wait_list(self, engine, deps):
        waits = []
        for t in list(deps) + self.gate:
            if t is None:
                continue
            if t.res == engine and not t.sync:
                continue
            wm = self.watermark[engine]
            if wm.get(t.res, 0) >= t.count:
                continue
            wm[t.res] = t.count
            waits.append((t.res, t.count))
        return waits

    def op(self, engine, fn, deps=(), sync_out=False):
        waits = self._wait_list(engine, deps)
        self.counts[engine] += 1
        tok = Tok(engine, self.counts[engine], sync=sync_out)
        self.ops[engine].append((fn, [(engine, 1)], waits))
        return tok

    def dma(self, chan, out, in_, deps=(), transpose=False):
        engine = self.chan_engine[chan]
        waits = self._wait_list(engine, deps)
        self.chan_count[chan] += 16
        tok = Tok(chan, self.chan_count[chan])

        def fn(eng, out=out, in_=in_, transpose=transpose):
            return eng.dma_start(out=out, in_=in_, transpose=transpose)

        self.ops[engine].append((fn, [(chan, 16)], waits))
        return tok

    def finalize(self):
        nc = self.nc
        import contextlib
        with contextlib.ExitStack() as stack:
            for res in list(ENGINES) + self.channels:
                self._sems[res] = stack.enter_context(nc.semaphore(f"s_{res}"))
            block = stack.enter_context(nc.Block())
            final_toks = [(cn, self.chan_count[cn]) for cn in self.channels
                          if self.chan_count[cn] > 0]

            def section(engine):
                def body(eng):
                    for fn, incs, waits in self.ops[engine]:
                        for res, cnt in waits:
                            eng.wait_ge(self._sems[res], cnt)
                        ins = fn(eng)
                        if ins is not None:
                            for res, amt in incs:
                                ins.then_inc(self._sems[res], amt)
                    if engine == "sync":
                        for res, cnt in final_toks:
                            eng.wait_ge(self._sems[res], cnt)
                return body

            for e in ENGINES:
                getattr(block, e)(section(e))


def _hadamard(n):
    h = np.array([[1.0]], dtype=np.float32)
    while h.shape[0] < n:
        h = np.block([[h, h], [h, -h]])
    return h.astype(np.float32)


def host_constants():
    return {"c_h128": _hadamard(128), "c_id": np.eye(128, dtype=np.float32)}


class Ctx:
    pass


def build_program(parts=("ffn1", "attn", "ffn2"), round_explicit=False, debug=False):
    nc = bass.Bass("TRN2", target_bir_lowering=False)
    s = Sched(nc)
    c = Ctx()
    c.nc, c.s = nc, s
    c.round_explicit = round_explicit

    x_in = nc.dram_tensor("x_in", [N, D], F32, kind="ExternalInput").ap()
    x_out = nc.dram_tensor("x_out", [N, D], F32, kind="ExternalOutput").ap()
    c.w_nat = {
        "qkv": nc.dram_tensor("w_qkv", [3 * D, D], F32, kind="ExternalInput").ap(),
        "proj": nc.dram_tensor("w_proj", [D, D], F32, kind="ExternalInput").ap(),
        "gate": nc.dram_tensor("w_gate", [FI, D], F32, kind="ExternalInput").ap(),
        "up": nc.dram_tensor("w_up", [FI, D], F32, kind="ExternalInput").ap(),
        "down": nc.dram_tensor("w_down", [D, FI], F32, kind="ExternalInput").ap(),
    }
    c.g_qkv = nc.dram_tensor("g_qkv", [D], F32, kind="ExternalInput").ap()
    c.g_proj = nc.dram_tensor("g_proj", [D], F32, kind="ExternalInput").ap()
    c.g_norm = nc.dram_tensor("g_norm", [D], F32, kind="ExternalInput").ap()
    c.s_att = nc.dram_tensor("s_att", [HEADS], F32, kind="ExternalInput").ap()
    c.c_h128_d = nc.dram_tensor("c_h128", [128, 128], F32, kind="ExternalInput").ap()
    c.c_id_d = nc.dram_tensor("c_id", [128, 128], F32, kind="ExternalInput").ap()

    c.tq = {}
    for name, (o, k) in [("qkv", (3 * D, D)), ("proj", (D, D)), ("gate", (FI, D)),
                         ("up", (FI, D)), ("down", (D, FI))]:
        c.tq[name] = nc.dram_tensor(f"tq_{name}", [o, k], BF16, kind="Internal").ap()
    xb1 = nc.dram_tensor("xb1", [N, D], F32, kind="Internal").ap()
    xb2 = nc.dram_tensor("xb2", [N, D], F32, kind="Internal").ap()
    c.inter_d = nc.dram_tensor("inter_d", [FI, N], BF16, kind="Internal").ap()
    c.qkv_d = nc.dram_tensor("qkv_d", [3 * D, N], BF16, kind="Internal").ap()
    c.row_d = nc.dram_tensor("row_d", [16, 128], F32, kind="Internal").ap()

    c.debug = debug
    c.dbg = {}
    if debug:
        c.dbg["scal"] = nc.dram_tensor("d_scal", [1, 64], F32, kind="ExternalOutput").ap()
        c.dbg["a1"] = nc.dram_tensor("d_a1", [128, NT], F32, kind="ExternalOutput").ap()
        c.dbg["n1F"] = nc.dram_tensor("d_n1F", [128, NT * D], BF16, kind="ExternalOutput").ap()
        c.dbg["inter"] = nc.dram_tensor("d_inter", [FI, 128], BF16, kind="ExternalOutput").ap()
        c.dbg["bf"] = nc.dram_tensor("d_bf", [128, FI], BF16, kind="ExternalOutput").ap()
        c.dbg["zq"] = nc.dram_tensor("d_zq", [128, FI], F32, kind="ExternalOutput").ap()
        c.dbg["qc"] = nc.dram_tensor("d_qc", [128, 8], F32, kind="ExternalOutput").ap()
        c.dbg["n2"] = nc.dram_tensor("d_n2", [128, FC * 512], BF16, kind="ExternalOutput").ap()
        c.dbg["beta"] = nc.dram_tensor("d_beta", [128, NT], F32, kind="ExternalOutput").ap()
        c.dbg["wg0"] = nc.dram_tensor("d_wg0", [128, 16 * 512], BF16, kind="ExternalOutput").ap()
        c.dbg["bcga"] = nc.dram_tensor("d_bcga", [128, N], F32, kind="ExternalOutput").ap()
        c.dbg["gate0"] = nc.dram_tensor("d_gate0", [128, 512], F32, kind="ExternalOutput").ap()
        c.dbg["a1row"] = nc.dram_tensor("d_a1row", [1, N], F32, kind="ExternalOutput").ap()
        c.dbg["hqF"] = nc.dram_tensor("d_hqF", [128, NT * D], BF16, kind="ExternalOutput").ap()
        c.dbg["qkv0"] = nc.dram_tensor("d_qkv0", [3 * D, 128], BF16, kind="ExternalOutput").ap()
        c.dbg["mask"] = nc.dram_tensor("d_mask", [128, HEADS * 128], F32, kind="ExternalOutput").ap()
        c.dbg["lamt"] = nc.dram_tensor("d_lamt", [128, HEADS * 128], BF16, kind="ExternalOutput").ap()
        c.dbg["lamr"] = nc.dram_tensor("d_lamr", [128, HEADS * 128], BF16, kind="ExternalOutput").ap()
        c.dbg["lamT"] = nc.dram_tensor("d_lamT", [128, HEADS], F32, kind="ExternalOutput").ap()
        c.dbg["outc"] = nc.dram_tensor("d_outc", [128, D], F32, kind="ExternalOutput").ap()
        c.dbg["z2F"] = nc.dram_tensor("d_z2F", [128, NT * D], BF16, kind="ExternalOutput").ap()
        c.dbg["scm"] = nc.dram_tensor("d_scm", [128, 128], BF16, kind="ExternalOutput").ap()
        c.dbg["vT"] = nc.dram_tensor("d_vT", [128, 128], BF16, kind="ExternalOutput").ap()
        c.dbg["slab"] = nc.dram_tensor("d_slab", [128, 48 * 128], BF16, kind="ExternalOutput").ap()
        c.dbg["Sb"] = nc.dram_tensor("d_Sb", [128, HEADS * 128], BF16, kind="ExternalOutput").ap()
        c.dbg["o0"] = nc.dram_tensor("d_o0", [128, 128], F32, kind="ExternalOutput").ap()
        c.dbg["qtb"] = nc.dram_tensor("d_qtb", [128, HEADS * 128], BF16, kind="ExternalOutput").ap()
    c.ch_in = [s.channel(f"in{i}") for i in range(2)]
    c.ch_w = [s.channel(f"w{i}", "scalar") for i in range(2)]
    c.ch_out = [s.channel(f"out{i}") for i in range(2)]
    c.ch_misc = s.channel("misc")
    c.ps_guard = None

    import contextlib
    with contextlib.ExitStack() as stack:
        sb = lambda name, shape, dt: stack.enter_context(nc.sbuf_tensor(name, shape, dt))
        c.ones1 = sb("ones1", [1, 128], F32)
        c.onescol = sb("onescol", [128, 1], F32)
        c.id_f = sb("id_f", [128, 128], F32)
        c.id_b = sb("id_b", [128, 128], BF16)
        c.h128b = sb("h128b", [128, 128], BF16)
        c.scal = sb("scal", [1, 64], F32)
        c.cols = sb("cols", [128, 64], F32)
        c.neg_s = sb("neg_s", [128, HEADS], F32)
        c.lamT = sb("lamT", [128, HEADS], F32)
        c.a1cols = sb("a1cols", [128, NT], F32)
        c.a2cols = sb("a2cols", [128, NT], F32)
        c.betacols = sb("betacols", [128, NT], F32)
        c.ps_small = stack.enter_context(nc.psum_tensor("ps_small", [128, 512], F32))

        setup_constants(c)
        setup_quantize_weights(c)
        s.set_gate(s.frontier())

        io = {"ffn1": (x_in, xb1), "attn": (xb1, xb2), "ffn2": (xb2, x_out)}
        for pi, p in enumerate(parts):
            a, b = io[p]
            if pi == 0:
                a = x_in
            if pi == len(parts) - 1:
                b = x_out
            if p == "attn":
                attn_sublayer(c, a, b)
            else:
                ffn_sublayer(c, a, b)
            s.set_gate(s.frontier())
        s.finalize()
    return nc



def bcast_col(c, src11, dst_col, deps, scale=None):
    s = c.s
    tm = s.op("tensor", lambda e: e.matmul(c.ps_small[:, 0:1], c.ones1[:, :], src11,
                                           start=True, stop=True),
              deps=list(deps) + [c.tok_ones] + ([c.ps_guard] if c.ps_guard else []))
    if scale is None:
        t2 = s.op("vector", lambda e: e.tensor_copy(dst_col, c.ps_small[:, 0:1]),
                  deps=[tm], sync_out=True)
    else:
        t2 = s.op("vector", lambda e: e.tensor_scalar(dst_col, c.ps_small[:, 0:1],
                                                      scale, None, ALU.mult), deps=[tm], sync_out=True)
    c.ps_guard = t2
    return t2


def setup_constants(c):
    s, nc = c.s, c.nc
    c.tok_ones = s.op("vector", lambda e: e.memset(c.ones1[:, :], 1.0))
    c.tok_onesc = s.op("vector", lambda e: e.memset(c.onescol[:, :], 1.0))
    t_id = s.dma(c.ch_misc, c.id_f[:], c.c_id_d[:, :])
    c.tok_id = s.op("vector", lambda e: e.tensor_copy(c.id_b[:], c.id_f[:]), deps=[t_id])
    c.tok_idf = t_id
    with nc.sbuf_tensor("h128f", [128, 128], F32) as h128f:
        th = s.dma(c.ch_misc, h128f[:], c.c_h128_d[:, :])
        c.tok_h = s.op("vector", lambda e: e.tensor_copy(c.h128b[:], h128f[:]), deps=[th])
    s.set_gate(s.frontier())
    with nc.sbuf_tensor("srow", [1, HEADS], F32) as srow:
        ts0 = s.dma(c.ch_misc, srow[:, :], c.s_att[:].rearrange("(o h) -> o h", o=1))
        tm = s.op("tensor", lambda e: e.matmul(c.ps_small[:, 0:HEADS], c.ones1[:, :],
                                               srow[:, :], start=True, stop=True),
                  deps=[ts0, c.tok_ones, c.tok_h])
        tneg = s.op("vector", lambda e: e.tensor_scalar(c.neg_s[:, :],
                                                        c.ps_small[:, 0:HEADS],
                                                        -1.0, None, ALU.mult), deps=[tm])
        tl = s.op("scalar", lambda e: e.activation(c.lamT[:, :], c.ps_small[:, 0:HEADS],
                                                   AF.Exp, scale=-128.0), deps=[tm])
        c.tok_negs, c.tok_lamT = tneg, tl
        c.ps_guard = tl
    s.set_gate(s.frontier())


def setup_quantize_weights(c):
    s, nc = c.s, c.nc
    mats = ["gate", "up", "down", "qkv", "proj"]
    c.mat_idx = {m: i for i, m in enumerate(mats)}
    c.tern_tok = {}
    import contextlib
    with contextlib.ExitStack() as _st:
        _e = _st.enter_context
        wt0 = _e(nc.sbuf_tensor("wt0", [128, 8192], F32))
        wt1 = _e(nc.sbuf_tensor("wt1", [128, 8192], F32))
        dump = _e(nc.sbuf_tensor("dumpw", [128, 8192], F32))
        cmp1 = _e(nc.sbuf_tensor("cmp1", [128, 8192], F32))
        tnb0 = _e(nc.sbuf_tensor("tnb0", [128, 8192], BF16))
        tnb1 = _e(nc.sbuf_tensor("tnb1", [128, 8192], BF16))
        acc = _e(nc.sbuf_tensor("accw", [128, 16], F32))
        wts, tnbs = [wt0, wt1], [tnb0, tnb1]
        for m in mats:
            i = c.mat_idx[m]
            w = c.w_nat[m]
            O, K = w.tensor.shape
            nelem = O * K
            cols = min(K, 8192)
            ntr, ntc = O // 128, K // cols
            t_acc0 = s.op("vector", lambda e, i=i: e.memset(acc[:, i:i + 1], 0.0))
            last = [t_acc0]
            prev_use = [None, None]
            for r in range(ntr):
                for cc in range(ntc):
                    b = (r * ntc + cc) % 2
                    tl = s.dma(c.ch_in[b], wts[b][:, 0:cols],
                               w[r * 128:(r + 1) * 128, cc * cols:(cc + 1) * cols],
                               deps=[prev_use[b]] if prev_use[b] else [])
                    ta = s.op("scalar", lambda e, b=b, i=i, cols=cols: e.activation(
                        dump[:, 0:cols], wts[b][:, 0:cols], AF.Abs,
                        accum_out=acc[:, 8 + i:9 + i]), deps=[tl] + last)
                    tacc = s.op("vector", lambda e, i=i: e.tensor_tensor(
                        acc[:, i:i + 1], acc[:, i:i + 1], acc[:, 8 + i:9 + i], ALU.add),
                        deps=[ta], sync_out=True)
                    prev_use[b] = ta
                    last = [tacc]
            tm = s.op("tensor", lambda e, i=i: e.matmul(
                c.ps_small[0:1, 0:1], acc[:, i:i + 1], c.onescol[:, :],
                start=True, stop=True),
                deps=last + [c.tok_onesc] + ([c.ps_guard] if c.ps_guard else []))
            tv = s.op("vector", lambda e, i=i, n=nelem: e.tensor_scalar(
                c.scal[0:1, 2 * i:2 * i + 1], c.ps_small[0:1, 0:1],
                1.0 / n, Q_EPS, ALU.mult, ALU.max), deps=[tm], sync_out=True)
            c.ps_guard = tv
            tt = s.op("vector", lambda e, i=i: e.tensor_scalar(
                c.scal[0:1, 2 * i + 1:2 * i + 2], c.scal[0:1, 2 * i:2 * i + 1],
                0.5, None, ALU.mult), deps=[tv], sync_out=True)
            tc1 = bcast_col(c, c.scal[0:1, 2 * i + 1:2 * i + 2],
                            c.cols[:, 4 * i:4 * i + 1], [tt])
            tc2 = s.op("vector", lambda e, i=i: e.tensor_scalar(
                c.cols[:, 4 * i + 1:4 * i + 2], c.cols[:, 4 * i:4 * i + 1],
                -1.0, None, ALU.mult), deps=[tc1], sync_out=True)
            tc3 = bcast_col(c, c.scal[0:1, 2 * i:2 * i + 1],
                            c.cols[:, 4 * i + 3:4 * i + 4], [tv])
            tc4 = s.op("vector", lambda e, i=i: e.tensor_scalar(
                c.cols[:, 4 * i + 2:4 * i + 3], c.cols[:, 4 * i + 3:4 * i + 4],
                -1.0, None, ALU.mult), deps=[tc3], sync_out=True)
            prev_use = [None, None]
            prev_st = [None, None]
            tern_last = [None, None]
            for r in range(ntr):
                for cc in range(ntc):
                    b = (r * ntc + cc) % 2
                    tl = s.dma(c.ch_in[b], wts[b][:, 0:cols],
                               w[r * 128:(r + 1) * 128, cc * cols:(cc + 1) * cols],
                               deps=([prev_use[b]] if prev_use[b] else [tc2, tc4]))
                    tg = s.op("vector", lambda e, b=b, i=i, cols=cols: e.tensor_scalar(
                        cmp1[:, 0:cols], wts[b][:, 0:cols],
                        c.cols[:, 4 * i:4 * i + 1], None, ALU.is_ge),
                        deps=[tl, tc2, tc4] + ([prev_st[b]] if prev_st[b] else []))
                    tn = s.op("vector", lambda e, b=b, i=i, cols=cols: e.scalar_tensor_tensor(
                        tnbs[b][:, 0:cols], wts[b][:, 0:cols],
                        c.cols[:, 4 * i + 1:4 * i + 2], cmp1[:, 0:cols],
                        ALU.is_le, ALU.subtract), deps=[tg])
                    prev_use[b] = tn
                    tst = s.dma(c.ch_out[b],
                                c.tq[m][r * 128:(r + 1) * 128, cc * cols:(cc + 1) * cols],
                                tnbs[b][:, 0:cols], deps=[tn])
                    prev_st[b] = tst
                    tern_last[b] = tst
            c.tern_tok[m] = [t for t in tern_last if t is not None] + [tc2, tc4]



def emit_ln(c, x_tile, y_tile, dump, colbuf, deps, eps=LN_EPS):
    s = c.s
    cs, cq, cm, cmsq, cvar, cstd, crstd, cneg = (colbuf[:, i:i + 1] for i in range(8))
    t1 = s.op("vector", lambda e: e.reduce_sum(cs, x_tile, AX), deps=deps, sync_out=True)
    t2 = s.op("scalar", lambda e: e.activation(dump, x_tile, AF.Square, accum_out=cq),
              deps=deps)
    t3 = s.op("vector", lambda e: e.tensor_scalar(cm, cs, 1.0 / D, None, ALU.mult),
              deps=[t1], sync_out=True)
    t4 = s.op("vector", lambda e: e.tensor_scalar(cmsq, cq, 1.0 / D, None, ALU.mult),
              deps=[t2], sync_out=True)
    t5 = s.op("vector", lambda e: e.tensor_mul(cvar, cm, cm), deps=[t3], sync_out=True)
    t6 = s.op("vector", lambda e: e.tensor_sub(cvar, cmsq, cvar), deps=[t4, t5], sync_out=True)
    t7 = s.op("vector", lambda e: e.tensor_scalar(cvar, cvar, eps, None, ALU.add),
              deps=[t6], sync_out=True)
    t8 = s.op("scalar", lambda e: e.activation(cstd, cvar, AF.Sqrt), deps=[t7])
    t9 = s.op("vector", lambda e: e.reciprocal(crstd, cstd), deps=[t8], sync_out=True)
    t10 = s.op("vector", lambda e: e.tensor_scalar(cneg, cm, crstd, -1.0,
                                                   ALU.mult, ALU.mult), deps=[t9], sync_out=True)
    t11 = s.op("scalar", lambda e: e.activation(y_tile, x_tile, AF.Identity,
                                                bias=cneg, scale=crstd), deps=[t10])
    return t11, [t1, t11]


def emit_quant(c, y_tile, n_bf_tile, i8_tile, colbuf, alpha_dst, deps):
    s = c.s
    cmx, cmxc, crec, cs1 = (colbuf[:, i:i + 1] for i in range(4))
    t1 = s.op("vector", lambda e: e.reduce_max(cmx, y_tile, AX,
                                               apply_absolute_value=True), deps=deps, sync_out=True)
    t2 = s.op("vector", lambda e: e.tensor_scalar(cmxc, cmx, Q_EPS, None, ALU.max),
              deps=[t1], sync_out=True)
    t3 = s.op("vector", lambda e: e.reciprocal(crec, cmxc), deps=[t2], sync_out=True)
    t4 = s.op("vector", lambda e: e.tensor_scalar(cs1, crec, 128.0, None, ALU.mult),
              deps=[t3], sync_out=True)
    ta = s.op("vector", lambda e: e.tensor_scalar(alpha_dst, cmxc, 1.0 / 128.0,
                                                  None, ALU.mult), deps=[t2], sync_out=True)
    t5 = s.op("vector", lambda e: e.tensor_scalar(i8_tile, y_tile, cs1, None, ALU.mult),
              deps=[t4])
    t6 = s.op("vector", lambda e: e.tensor_copy(n_bf_tile, i8_tile), deps=[t5])
    return t6, ta


def emit_transpose_quad(c, src_bf, dst_view_fn, ps_pair, prev_ev, dep, tag):
    s = c.s
    out_toks = []
    for g in range(4):
        pb = ps_pair[g % 2]
        trs = []
        for j2 in range(4):
            j = g * 4 + j2
            trs.append(s.op("tensor", lambda e, j=j, j2=j2, pb=pb: e.transpose(
                pb[:, j2 * 128:(j2 + 1) * 128], src_bf[:, j * 128:(j + 1) * 128],
                c.id_b[:]),
                deps=[dep, c.tok_id] + ([prev_ev[g % 2]] if prev_ev[g % 2] else [])))
        tev = s.op("vector", lambda e, g=g, pb=pb: e.tensor_copy(
            dst_view_fn(g), pb[:, :].rearrange("p (j t) -> p j t", t=128)), deps=trs)
        prev_ev[g % 2] = tev
        out_toks.append(tev)
    return out_toks


def emit_row_from_cols(c, cols_ap, row_ap, nchunk, deps):
    s, nc = c.s, _proxy(c)
    ttr = s.op("tensor", lambda e: e.transpose(
        c.ps_small[0:nchunk, 0:128], cols_ap, c.id_f[:]),
        deps=list(deps) + [c.tok_idf] + ([c.ps_guard] if c.ps_guard else []))
    with nc.sbuf_tensor("rowT", [nchunk, 128], F32) as rowT:
        tev = s.op("vector", lambda e: e.tensor_copy(rowT[:, :],
                                                     c.ps_small[0:nchunk, 0:128]),
                   deps=[ttr])
        c.ps_guard = tev
        tst = s.dma(c.ch_misc, c.row_d[0:nchunk, :], rowT[:, :], deps=[tev])
        tld = s.dma(c.ch_misc, row_ap,
                    c.row_d[0:nchunk, :].rearrange("(o i) p -> o (i p)", o=1), deps=[tst])
    s.set_gate(s.frontier())
    return tld


def emit_bc_rows(c, row_ap, dst, scale_col, deps, n=N):
    s = c.s
    toks = []
    for seg in range(n // 512):
        tm = s.op("tensor", lambda e, seg=seg: e.matmul(
            c.ps_small[:, 0:512], c.ones1[:, :], row_ap[0:1, seg * 512:(seg + 1) * 512],
            start=True, stop=True),
            deps=list(deps) + [c.tok_ones] + ([c.ps_guard] if c.ps_guard else []))
        if scale_col is None:
            tev = s.op("vector", lambda e, seg=seg: e.tensor_copy(
                dst[:, seg * 512:(seg + 1) * 512], c.ps_small[:, 0:512]), deps=[tm])
        else:
            tev = s.op("vector", lambda e, seg=seg: e.tensor_scalar(
                dst[:, seg * 512:(seg + 1) * 512], c.ps_small[:, 0:512],
                scale_col, None, ALU.mult), deps=[tm])
        c.ps_guard = tev
        toks.append(tev)
    return toks





class _NCProxy:
    _n = [0]

    def __init__(self, nc):
        self._nc = nc
        _NCProxy._n[0] += 1
        self._tag = "_u%d" % _NCProxy._n[0]

    def sbuf_tensor(self, name, shape, dt):
        return self._nc.sbuf_tensor(name + self._tag, shape, dt)

    def psum_tensor(self, name, shape, dt):
        return self._nc.psum_tensor(name + self._tag, shape, dt)


def _proxy(c):
    return _NCProxy(c.nc)

def ffn_sublayer(c, in_ap, out_ap):
    s, nc = c.s, _proxy(c)
    ig, iu = c.mat_idx["gate"], c.mat_idx["up"]
    import contextlib as _cl
    with _cl.ExitStack() as _st0:
        _e0 = _st0.enter_context
        n1F = _e0(nc.sbuf_tensor("n1F", [128, NT * D], BF16))
        bcga = _e0(nc.sbuf_tensor("bcga", [128, N], F32))
        bcua = _e0(nc.sbuf_tensor("bcua", [128, N], F32))
        c.a1row = _e0(nc.sbuf_tensor("a1row", [1, N], F32))
        import contextlib as _cl
        with _cl.ExitStack() as _st1:
            _e1 = _st1.enter_context
            xt0 = _e1(nc.sbuf_tensor("xt0", [128, D], F32))
            xt1 = _e1(nc.sbuf_tensor("xt1", [128, D], F32))
            y1t = _e1(nc.sbuf_tensor("y1", [128, D], F32))
            dmp = _e1(nc.sbuf_tensor("dmp", [128, D], F32))
            i8t = _e1(nc.sbuf_tensor("i8t", [128, D], I8))
            nbt = _e1(nc.sbuf_tensor("nbt", [128, D], BF16))
            lncol = _e1(nc.sbuf_tensor("lncol", [128, 16], F32))
            ps_tr0 = _e1(nc.psum_tensor("ps_tr0", [128, 512], BF16))
            ps_tr1 = _e1(nc.psum_tensor("ps_tr1", [128, 512], BF16))
            xts = [xt0, xt1]
            prev_rd = [None, None]
            prev_ev = [None, None]
            for i in range(NT):
                b = i % 2
                tl = s.dma(c.ch_in[b], xts[b][:], in_ap[i * 128:(i + 1) * 128, :],
                           deps=[prev_rd[b]] if prev_rd[b] else [])
                ty, readers = emit_ln(c, xts[b][:], y1t[:], dmp[:], lncol, [tl])
                prev_rd[b] = readers[0]
                prev_rd[b] = ty if b == (i % 2) else prev_rd[b]
                tq_, ta = emit_quant(c, y1t[:], nbt[:], i8t[:], lncol[:, 8:12],
                                     c.a1cols[:, i:i + 1], [ty])
                emit_transpose_quad(
                    c, nbt,
                    lambda g, i=i: n1F[:, :].rearrange("p (j t) -> p j t", t=D)[
                        :, 4 * g:4 * g + 4, i * 128:(i + 1) * 128],
                    [ps_tr0, ps_tr1], prev_ev, tq_, f"n1F{i}")
            p1_done = s.frontier()
        s.set_gate(p1_done)
        if c.debug and not getattr(c, "dbg_done", False):
            s.dma(c.ch_misc, c.dbg["a1"][:, :], c.a1cols[:, :], deps=p1_done)
            s.dma(c.ch_misc, c.dbg["n1F"][:, :], n1F[:, :], deps=p1_done)

        tld = emit_row_from_cols(c, c.a1cols[:, :], c.a1row[:, :], NT, p1_done)
        if c.debug and not getattr(c, "dbg_done", False):
            s.dma(c.ch_misc, c.dbg["a1row"][:, :], c.a1row[:, :], deps=[tld])
        bc1 = emit_bc_rows(c, c.a1row, bcga, c.cols[:, 4 * ig + 2:4 * ig + 3], [tld])
        bc2 = emit_bc_rows(c, c.a1row, bcua, c.cols[:, 4 * iu + 2:4 * iu + 3], [tld])
        bc_toks = bc1 + bc2

        import contextlib as _cl
        with _cl.ExitStack() as _st2:
            _e2 = _st2.enter_context
            wg0 = _e2(nc.sbuf_tensor("wg0", [128, 16 * 512], BF16))
            wg1 = _e2(nc.sbuf_tensor("wg1", [128, 16 * 512], BF16))
            wu0 = _e2(nc.sbuf_tensor("wu0", [128, 16 * 512], BF16))
            wu1 = _e2(nc.sbuf_tensor("wu1", [128, 16 * 512], BF16))
            ev1 = _e2(nc.sbuf_tensor("ev1", [128, 512], F32))
            ev2 = _e2(nc.sbuf_tensor("ev2", [128, 512], F32))
            ev3 = _e2(nc.sbuf_tensor("ev3", [128, 512], F32))
            ist0 = _e2(nc.sbuf_tensor("ist0", [128, N], BF16))
            ist1 = _e2(nc.sbuf_tensor("ist1", [128, N], BF16))
            ps_g0 = _e2(nc.psum_tensor("ps_g0", [128, 512], F32))
            ps_g1 = _e2(nc.psum_tensor("ps_g1", [128, 512], F32))
            ps_u0 = _e2(nc.psum_tensor("ps_u0", [128, 512], F32))
            ps_u1 = _e2(nc.psum_tensor("ps_u1", [128, 512], F32))
            wgs, wus = [wg0, wg1], [wu0, wu1]
            psg, psu = [ps_g0, ps_g1], [ps_u0, ps_u1]
            ists = [ist0, ist1]

            def load_group(og, b, deps):
                toks = []
                for f in range(16):
                    toks.append(s.dma(c.ch_w[0], wgs[b][:, f * 512:(f + 1) * 512],
                                      c.tq["gate"][og * 512:(og + 1) * 512,
                                                   f * 128:(f + 1) * 128],
                                      deps=deps, transpose=True))
                    toks.append(s.dma(c.ch_w[1], wus[b][:, f * 512:(f + 1) * 512],
                                      c.tq["up"][og * 512:(og + 1) * 512,
                                                 f * 128:(f + 1) * 128],
                                      deps=deps, transpose=True))
                return toks

            if c.debug and not getattr(c, "dbg_done", False):
                pass
            wslab_tok = [None, None]
            last_mm_for_slab = [None, None]
            prev_evac = {0: None, 1: None}
            prev_ist_dma = [None, None]
            wdeps = list(c.tern_tok["gate"]) + list(c.tern_tok["up"])
            wslab_tok[0] = load_group(0, 0, wdeps)
            if c.debug and not getattr(c, "dbg_done", False):
                s.dma(c.ch_misc, c.dbg["wg0"][:, :], wg0[:, :], deps=wslab_tok[0])
                s.dma(c.ch_misc, c.dbg["bcga"][:, :], bcga[:, :], deps=bc_toks)
            for og in range(16):
                b = og % 2
                if og + 1 < 16:
                    nb = (og + 1) % 2
                    wslab_tok[nb] = load_group(
                        og + 1, nb,
                        ([last_mm_for_slab[nb]] if last_mm_for_slab[nb] else []) + wdeps)
                for oc4 in range(4):
                    oc = og * 4 + oc4
                    ofs = oc4 * 128
                    for tg in range(4):
                        par = (oc4 * 4 + tg) % 2
                        depf = wslab_tok[b] + ([prev_evac[par]] if prev_evac[par] else [])
                        mm_g = mm_u = None
                        for f in range(16):
                            mm_g = s.op("tensor", lambda e, f=f, b=b, tg=tg, par=par,
                                        ofs=ofs: e.matmul(
                                psg[par][:, :],
                                wgs[b][:, f * 512 + ofs:f * 512 + ofs + 128],
                                n1F[:, f * D + tg * 512:f * D + (tg + 1) * 512],
                                start=(f == 0), stop=(f == 15)),
                                deps=(depf if f == 0 else []))
                        for f in range(16):
                            mm_u = s.op("tensor", lambda e, f=f, b=b, tg=tg, par=par,
                                        ofs=ofs: e.matmul(
                                psu[par][:, :],
                                wus[b][:, f * 512 + ofs:f * 512 + ofs + 128],
                                n1F[:, f * D + tg * 512:f * D + (tg + 1) * 512],
                                start=(f == 0), stop=(f == 15)),
                                deps=(depf if f == 0 else []))
                        last_mm_for_slab[b] = mm_u
                        te1 = s.op("vector", lambda e, par=par, tg=tg: e.tensor_mul(
                            ev1[:], psg[par][:, :], bcga[:, tg * 512:(tg + 1) * 512]),
                            deps=[mm_g] + bc_toks)
                        if c.debug and og == 0 and oc4 == 0 and tg == 0 and not getattr(c, "dbg_done", False):
                            s.dma(c.ch_misc, c.dbg["gate0"][:, :], ev1[:], deps=[te1])
                            te1 = Tok(c.ch_misc, s.chan_count[c.ch_misc])
                        te2 = s.op("scalar", lambda e: e.activation(ev2[:], ev1[:],
                                                                    AF.Silu), deps=[te1])
                        te3 = s.op("vector", lambda e, par=par, tg=tg: e.tensor_mul(
                            ev3[:], psu[par][:, :], bcua[:, tg * 512:(tg + 1) * 512]),
                            deps=[mm_u] + bc_toks)
                        te4 = s.op("vector", lambda e, oc=oc, tg=tg: e.tensor_mul(
                            ists[oc % 2][:, tg * 512:(tg + 1) * 512], ev2[:], ev3[:]),
                            deps=[te2, te3] + ([prev_ist_dma[oc % 2]]
                                               if (prev_ist_dma[oc % 2] and tg == 0) else []))
                        prev_evac[par] = te4
                    tso = s.dma(c.ch_out[oc % 2], c.inter_d[oc * 128:(oc + 1) * 128, :],
                                ists[oc % 2][:], deps=[prev_evac[0], prev_evac[1]])
                    prev_ist_dma[oc % 2] = tso
            p2_done = s.frontier()
        if c.debug and not getattr(c, "dbg_done", False):
            s.dma(c.ch_misc, c.dbg["scal"][:, :], c.scal[:, :], deps=p2_done)
            s.dma(c.ch_misc, c.dbg["inter"][:, :], c.inter_d[:, 0:128], deps=p2_done)
    s.set_gate(p2_done)
    ffn_p34(c, in_ap, out_ap)
    c.dbg_done = True


def ffn_p34(c, in_ap, out_ap):
    s, nc = c.s, _proxy(c)
    idn = c.mat_idx["down"]
    import contextlib as _cl
    with _cl.ExitStack() as _st3:
        _e3 = _st3.enter_context
        n2sb = _e3(nc.sbuf_tensor("n2sb", [128, FC * 512], BF16))
        interb = _e3(nc.sbuf_tensor("interb", [128, FI], BF16))
        zq = _e3(nc.sbuf_tensor("zq", [128, FI], F32))
        mscr = _e3(nc.sbuf_tensor("mscr", [128, 4096], F32))
        mscr2 = _e3(nc.sbuf_tensor("mscr2", [128, 4096], F32))
        i8q = _e3(nc.sbuf_tensor("i8q", [128, 4 * FI], I8))
        s2r = _e3(nc.sbuf_tensor("s2r", [1, 640], F32))
        s2quad = _e3(nc.sbuf_tensor("s2quad", [128, 512], F32))
        qcol = _e3(nc.sbuf_tensor("qcol", [128, 8], F32))
        wdr = _e3(nc.sbuf_tensor("wdr", [128, 8 * 512], BF16))
        xsl = _e3(nc.sbuf_tensor("xsl", [128, 4 * 512], F32))
        ot0 = _e3(nc.sbuf_tensor("ot0", [128, 512], F32))
        ot1 = _e3(nc.sbuf_tensor("ot1", [128, 512], F32))
        ps_h0 = _e3(nc.psum_tensor("ps_h0", [128, 512], F32))
        ps_h1 = _e3(nc.psum_tensor("ps_h1", [128, 512], F32))
        ps_t = _e3(nc.psum_tensor("ps_t", [128, 512], F32))
        ps_d0 = _e3(nc.psum_tensor("ps_d0", [128, 512], F32))
        ps_d1 = _e3(nc.psum_tensor("ps_d1", [128, 512], F32))
        ps_d2 = _e3(nc.psum_tensor("ps_d2", [128, 512], F32))
        ps_d3 = _e3(nc.psum_tensor("ps_d3", [128, 512], F32))
        ps_h = [ps_h0, ps_h1]
        ps_d = [ps_d0, ps_d1, ps_d2, ps_d3]
        ots = [ot0, ot1]
        st = {"interb": None, "zq_ev": [None, None], "i8": {}, "pst": None,
              "n2done": {}, "p4mm_last": None, "p4_out": {}, "ring": [None] * 8,
              "ring_use": [None] * 8, "x": [None] * 4}

        def p3_load_bf(blk):
            toks = []
            blk_dbg = blk
            for piece in range(4):
                toks.append(s.dma(
                    c.ch_in[blk % 2],
                    interb[:, piece * 2048:(piece + 1) * 2048].rearrange(
                        "p (cc t) -> p cc t", t=128),
                    c.inter_d[piece * 2048:(piece + 1) * 2048,
                              blk * 128:(blk + 1) * 128].rearrange(
                        "(cc p) t -> p cc t", p=128),
                    deps=([st["interb"]] if st["interb"] else [])
                    + ([st.get("interb_rd")] if st.get("interb_rd") else [])))
            tprev = toks
            for stg in range(6):
                h = 32 >> stg
                va = interb[:, :].rearrange("p (bb two c t) -> p bb two (c t)",
                                            two=2, c=h, t=128)
                a_view = va[:, :, 0, :]
                b_view = va[:, :, 1, :]
                t1 = s.op("vector", lambda e, a=a_view, b=b_view: e.tensor_tensor(
                    a, a, b, ALU.add), deps=tprev)
                t2 = s.op("vector", lambda e, a=a_view, b=b_view: e.scalar_tensor_tensor(
                    b, b, -2.0, a, ALU.mult, ALU.add), deps=[t1])
                tprev = [t2]
            st["interb"] = tprev[0]
            if c.debug and blk == 0 and not getattr(c, "dbg_done", False):
                st["interb"] = s.dma(c.ch_misc, c.dbg["bf"][:, :], interb[:, :],
                                     deps=tprev)
            return tprev[0]

        def p3_post(blk, bt_done):
            ev_toks = []
            for g in range(16):
                pb = ps_h[g % 2]
                prev = st["zq_ev"][g % 2]
                tm = s.op("tensor", lambda e, g=g, pb=pb: e.matmul(
                    pb[:, :], c.h128b[:], interb[:, g * 512:(g + 1) * 512],
                    start=True, stop=True),
                    deps=[bt_done, c.tok_h] + ([prev] if prev else []))
                st["interb_rd"] = tm
                tev = s.op("vector", lambda e, g=g, pb=pb: e.tensor_copy(
                    zq[:, g * 512:(g + 1) * 512], pb[:, :]), deps=[tm])
                st["zq_ev"][g % 2] = tev
                ev_toks.append(tev)
            if c.debug and blk == 0 and not getattr(c, "dbg_done", False):
                ev_toks = ev_toks + [s.dma(c.ch_misc, c.dbg["zq"][:, :], zq[:, :],
                                           deps=ev_toks)]
            tm1 = s.op("vector", lambda e: e.tensor_tensor(
                mscr[:, :], zq[:, 0:4096], zq[:, 4096:8192], ALU.max), deps=ev_toks,
                sync_out=True)
            tm1b = s.op("vector", lambda e: e.tensor_tensor(
                mscr2[:, :], zq[:, 0:4096], zq[:, 4096:8192], ALU.min), deps=ev_toks,
                sync_out=True)
            tprev, tprevb = tm1, tm1b
            w = 2048
            while w >= 128:
                tprev = s.op("vector", lambda e, w=w: e.tensor_tensor(
                    mscr[:, 0:w], mscr[:, 0:w], mscr[:, w:2 * w], ALU.max),
                    deps=[tprev], sync_out=True)
                tprevb = s.op("vector", lambda e, w=w: e.tensor_tensor(
                    mscr2[:, 0:w], mscr2[:, 0:w], mscr2[:, w:2 * w], ALU.min),
                    deps=[tprevb], sync_out=True)
                w //= 2
            tt = s.op("tensor", lambda e: e.transpose(ps_t[:, 0:128], mscr[:, 0:128],
                                                      c.id_f[:]),
                      deps=[tprev, c.tok_idf] + ([st["pst"]] if st["pst"] else []))
            tmxa = s.op("vector", lambda e: e.reduce_max(qcol[:, 4:5], ps_t[:, 0:128], AX),
                        deps=[tt], sync_out=True)
            ttb = s.op("tensor", lambda e: e.transpose(ps_t[:, 128:256], mscr2[:, 0:128],
                                                       c.id_f[:]), deps=[tprevb, tt, tmxa])
            tmxb = s.op("vector", lambda e: e.tensor_reduce(
                qcol[:, 5:6], ps_t[:, 128:256], AX, ALU.min), deps=[ttb], sync_out=True)
            tmxc = s.op("vector", lambda e: e.tensor_scalar(
                qcol[:, 6:7], qcol[:, 5:6], -1.0, None, ALU.mult), deps=[tmxb], sync_out=True)
            tmx = s.op("vector", lambda e: e.tensor_max(
                qcol[:, 0:1], qcol[:, 4:5], qcol[:, 6:7]), deps=[tmxa, tmxc], sync_out=True)
            tc1 = s.op("vector", lambda e: e.tensor_scalar(
                qcol[:, 1:2], qcol[:, 0:1], C8192, Q_EPS, ALU.mult, ALU.max), deps=[tmx], sync_out=True)
            tc2 = s.op("vector", lambda e: e.reciprocal(qcol[:, 2:3], qcol[:, 1:2]),
                       deps=[tc1], sync_out=True)
            tc3 = s.op("vector", lambda e: e.tensor_scalar(
                qcol[:, 3:4], qcol[:, 2:3], 128.0 * C8192, None, ALU.mult), deps=[tc2], sync_out=True)
            ta2 = s.op("vector", lambda e, blk=blk: e.tensor_scalar(
                c.a2cols[:, blk:blk + 1], qcol[:, 1:2], 1.0 / 128.0, None, ALU.mult),
                deps=[tc1], sync_out=True)
            tbeta = s.op("vector", lambda e, blk=blk, idn=idn: e.tensor_scalar(
                c.betacols[:, blk:blk + 1], c.a2cols[:, blk:blk + 1],
                c.cols[:, 4 * idn + 2:4 * idn + 3], 0.5, ALU.mult, ALU.mult),
                deps=[ta2], sync_out=True)
            tt2 = s.op("tensor", lambda e: e.transpose(ps_t[0:1, 256:384], qcol[:, 3:4],
                                                       c.id_f[:]),
                       deps=[tc3, c.tok_idf, tmx, tmxb])
            st["pst"] = tt2
            tr1 = s.op("vector", lambda e: e.tensor_copy(s2r[0:1, 0:128],
                                                         ps_t[0:1, 256:384]), deps=[tt2], sync_out=True)
            tq4 = []
            for q4 in range(4):
                tq4.append(s.op("vector", lambda e, q4=q4: e.tensor_copy(
                    s2r[0:1, 128 + q4 * 128:256 + q4 * 128], s2r[0:1, 0:128]),
                    deps=[tr1]))
            tmb = s.op("tensor", lambda e: e.matmul(ps_t[:, 0:512], c.ones1[:, :],
                                                    s2r[0:1, 128:640], start=True,
                                                    stop=True),
                       deps=tq4 + [c.tok_ones, tr1, tmx])
            st["pst"] = tmb
            tbc = s.op("vector", lambda e: e.tensor_copy(s2quad[:, :], ps_t[:, 0:512]),
                       deps=[tmb])
            st["pst"] = tbc
            big_toks = []
            for r8 in range(8):
                big_toks.append(s.op("vector", lambda e, r8=r8: e.tensor_copy(
                    mscr[:, r8 * 512:(r8 + 1) * 512], s2quad[:, :]), deps=[tbc]))
            hh = []
            for hf in range(2):
                hh.append(s.op("vector", lambda e, hf=hf: e.tensor_tensor(
                    zq[:, hf * 4096:(hf + 1) * 4096], zq[:, hf * 4096:(hf + 1) * 4096],
                    mscr[:, :], ALU.mult), deps=big_toks))
            if c.debug and blk == 0 and not getattr(c, "dbg_done", False):
                s.dma(c.ch_misc, c.dbg["qc"][:, :], qcol[:, :], deps=hh + [tbeta])
            ti8 = s.op("vector", lambda e, blk=blk: e.tensor_copy(
                i8q[:, (blk % 4) * FI:(blk % 4 + 1) * FI], zq[:, :]),
                deps=hh + [st["i8"].get(blk % 4)] +
                ([Tok(c.ch_misc, s.chan_count[c.ch_misc])] if (c.debug and blk == 0) else []))
            st["i8"][blk % 4] = ti8
            return ti8

        def p3_n2copy(blk, ti8, extra):
            qi = blk % 4
            srcv = i8q[:, qi * FI:(qi + 1) * FI].rearrange("p (cc t) -> p cc t", t=128)
            dstv = n2sb[:, :].rearrange("p (cc t) -> p cc t", t=512)[
                :, :, qi * 128:(qi + 1) * 128]
            t = s.op("vector", lambda e, srcv=srcv, dstv=dstv: e.tensor_copy(dstv, srcv),
                     deps=[ti8] + extra)
            st["n2done"][blk] = t
            return t

        def p4_ob(q, ob, n2toks):
            mm_last = [None] * 4
            xdma = [None] * 4
            for tc in range(4):
                xdma[tc] = s.dma(c.ch_in[tc % 2], xsl[:, tc * 512:(tc + 1) * 512],
                                 in_ap[(q * 4 + tc) * 128:(q * 4 + tc + 1) * 128,
                                       ob * 512:(ob + 1) * 512],
                                 deps=[st["x"][tc]] if st["x"][tc] else [])
            for f in range(64):
                slot = f % 8
                st["ring"][slot] = s.dma(
                    c.ch_w[f % 2], wdr[:, slot * 512:(slot + 1) * 512],
                    c.tq["down"][ob * 512:(ob + 1) * 512, f * 128:(f + 1) * 128],
                    deps=([st["ring_use"][slot]] if st["ring_use"][slot] else [])
                    + list(c.tern_tok["down"]), transpose=True)
                for tc in range(4):
                    mm_last[tc] = s.op("tensor", lambda e, f=f, tc=tc, slot=slot: e.matmul(
                        ps_d[tc][:, :],
                        n2sb[:, f * 512 + tc * 128:f * 512 + (tc + 1) * 128],
                        wdr[:, slot * 512:(slot + 1) * 512],
                        start=(f == 0), stop=(f == 63)),
                        deps=[st["ring"][slot]] + (n2toks if f == 0 else [])
                        + ([st["p4_out"].get(("war", tc))] if (f == 0 and
                           st["p4_out"].get(("war", tc))) else []))
                st["ring_use"][slot] = mm_last[3]
            st["p4mm_last"] = mm_last[3]
            return mm_last, xdma

        def p4_ob_evac(q, ob, mm_last, xdma):
            for tc in range(4):
                par = tc % 2
                tev = s.op("vector", lambda e, tc=tc, q=q, par=par: e.scalar_tensor_tensor(
                    ots[par][:], ps_d[tc][:, :],
                    c.betacols[:, q * 4 + tc:q * 4 + tc + 1],
                    xsl[:, tc * 512:(tc + 1) * 512], ALU.mult, ALU.add),
                    deps=[mm_last[tc], xdma[tc]]
                    + ([st["p4_out"].get(("dma", par))] if st["p4_out"].get(("dma", par)) else []))
                tst = s.dma(c.ch_out[par],
                            out_ap[(q * 4 + tc) * 128:(q * 4 + tc + 1) * 128,
                                   ob * 512:(ob + 1) * 512], ots[par][:], deps=[tev])
                st["x"][tc] = tev
                st["p4_out"][("war", tc)] = tev
                st["p4_out"][("dma", par)] = tst

        for bi in range(4):
            bt = p3_load_bf(bi)
            ti8 = p3_post(bi, bt)
            p3_n2copy(bi, ti8, [])
        for q in range(4):
            n2toks = [st["n2done"][q * 4 + bi] for bi in range(4)]
            if c.debug and q == 0 and not getattr(c, "dbg_done", False):
                s.dma(c.ch_misc, c.dbg["n2"][:, :], n2sb[:, :], deps=n2toks)
                s.dma(c.ch_misc, c.dbg["beta"][:, :], c.betacols[:, :], deps=n2toks)
                n2toks = n2toks + [Tok(c.ch_misc, s.chan_count[c.ch_misc])]
            for ob in range(4):
                mm_last, xdma = p4_ob(q, ob, n2toks if ob == 0 else [])
                if q + 1 < 4:
                    blk = (q + 1) * 4 + ob
                    bt = p3_load_bf(blk)
                    p4_ob_evac(q, ob, mm_last, xdma)
                    p3_post(blk, bt)
                else:
                    p4_ob_evac(q, ob, mm_last, xdma)
            if q + 1 < 4:
                gatep4 = [Tok("tensor", s.counts["tensor"])]
                for bi in range(4):
                    blk = (q + 1) * 4 + bi
                    p3_n2copy(blk, st["i8"][blk % 4], gatep4)



def attn_sublayer(c, in_ap, out_ap):
    s, nc = c.s, _proxy(c)
    iq = c.mat_idx["qkv"]
    import contextlib as _cl
    with _cl.ExitStack() as _st4:
        _e4 = _st4.enter_context
        gqkv_bc = _e4(nc.sbuf_tensor("gqkv_bc", [128, D], F32))
        gn_bc = _e4(nc.sbuf_tensor("gn_bc", [128, D], F32))
        gp_bc = _e4(nc.sbuf_tensor("gp_bc", [128, D], F32))
        bc_all = []
        tprev = None
        with nc.sbuf_tensor("grow", [1, D], F32) as grow:
            for vec, dst in [(c.g_qkv, gqkv_bc), (c.g_norm, gn_bc), (c.g_proj, gp_bc)]:
                tl = s.dma(c.ch_misc, grow[:, :], vec[:].rearrange("(o d) -> o d", o=1),
                           deps=[tprev] if tprev else [])
                toks = emit_bc_rows(c, grow, dst, None, [tl])
                tprev = toks[-1]
                bc_all += toks
        s.set_gate(s.frontier())

        with nc.sbuf_tensor("hqF", [128, NT * D], BF16) as hqF:
            import contextlib as _cl
            with _cl.ExitStack() as _st5:
                _e5 = _st5.enter_context
                xt0 = _e5(nc.sbuf_tensor("xta", [128, D], F32))
                xt1 = _e5(nc.sbuf_tensor("xtb", [128, D], F32))
                y2t = _e5(nc.sbuf_tensor("y2", [128, D], F32))
                hqt = _e5(nc.sbuf_tensor("hq", [128, D], F32))
                hqbt = _e5(nc.sbuf_tensor("hqb", [128, D], BF16))
                dmp = _e5(nc.sbuf_tensor("dmpa", [128, D], F32))
                lncol = _e5(nc.sbuf_tensor("lncola", [128, 16], F32))
                ps_tr0 = _e5(nc.psum_tensor("ps_tra0", [128, 512], BF16))
                ps_tr1 = _e5(nc.psum_tensor("ps_tra1", [128, 512], BF16))
                xts = [xt0, xt1]
                prev_rd = [None, None]
                prev_ev = [None, None]
                for i in range(NT):
                    b = i % 2
                    tl = s.dma(c.ch_in[b], xts[b][:], in_ap[i * 128:(i + 1) * 128, :],
                               deps=[prev_rd[b]] if prev_rd[b] else [])
                    ty, readers = emit_ln(c, xts[b][:], y2t[:], dmp[:], lncol, [tl])
                    prev_rd[b] = ty
                    t1 = s.op("scalar", lambda e: e.activation(
                        dmp[:], y2t[:], AF.Square, accum_out=lncol[:, 8:9]), deps=[ty])
                    t2 = s.op("vector", lambda e: e.tensor_scalar(
                        lncol[:, 9:10], lncol[:, 8:9], 1.0 / D, EPS_A, ALU.mult, ALU.add),
                        deps=[t1], sync_out=True)
                    t3 = s.op("scalar", lambda e: e.activation(
                        lncol[:, 10:11], lncol[:, 9:10], AF.Sqrt), deps=[t2])
                    t4 = s.op("vector", lambda e: e.reciprocal(
                        lncol[:, 11:12], lncol[:, 10:11]), deps=[t3], sync_out=True)
                    t5 = s.op("vector", lambda e: e.scalar_tensor_tensor(
                        hqt[:], y2t[:], lncol[:, 11:12], gqkv_bc[:, :],
                        ALU.mult, ALU.mult), deps=[t4] + bc_all)
                    t6 = s.op("vector", lambda e: e.tensor_copy(hqbt[:], hqt[:]),
                              deps=[t5])
                    emit_transpose_quad(
                        c, hqbt,
                        lambda g, i=i: hqF[:, :].rearrange("p (j t) -> p j t", t=D)[
                        :, 4 * g:4 * g + 4, i * 128:(i + 1) * 128],
                        [ps_tr0, ps_tr1], prev_ev, t6, f"hqF{i}")
                a1_done = s.frontier()
                if c.debug:
                    s.dma(c.ch_misc, c.dbg["hqF"][:, :], hqF[:, :], deps=a1_done)
            s.set_gate(a1_done)

            import contextlib as _cl
            with _cl.ExitStack() as _st6:
                _e6 = _st6.enter_context
                wq0 = _e6(nc.sbuf_tensor("wq0", [128, 16 * 512], BF16))
                wq1 = _e6(nc.sbuf_tensor("wq1", [128, 16 * 512], BF16))
                eva0 = _e6(nc.sbuf_tensor("eva0", [128, 512], BF16))
                eva1 = _e6(nc.sbuf_tensor("eva1", [128, 512], BF16))
                ps_q0 = _e6(nc.psum_tensor("ps_q0", [128, 512], F32))
                ps_q1 = _e6(nc.psum_tensor("ps_q1", [128, 512], F32))
                wqs, evs, psq = [wq0, wq1], [eva0, eva1], [ps_q0, ps_q1]
                slab_tok = [None, None]
                last_mm = [None, None]
                prev_ev2 = [None, None]

                def load_qgroup(og, b, deps):
                    toks = []
                    for f in range(16):
                        toks.append(s.dma(c.ch_w[f % 2],
                                          wqs[b][:, f * 512:(f + 1) * 512],
                                          c.tq["qkv"][og * 512:(og + 1) * 512,
                                                      f * 128:(f + 1) * 128],
                                          deps=deps, transpose=True))
                    return toks

                slab_tok[0] = load_qgroup(0, 0, list(c.tern_tok["qkv"]))
                for og in range(12):
                    b = og % 2
                    if og + 1 < 12:
                        nb = (og + 1) % 2
                        slab_tok[nb] = load_qgroup(
                            og + 1, nb,
                            ([last_mm[nb]] if last_mm[nb] else []) + list(c.tern_tok["qkv"]))
                    for oc4 in range(4):
                        oc = og * 4 + oc4
                        ofs = oc4 * 128
                        for tg in range(4):
                            par = (oc4 * 4 + tg) % 2
                            depf = slab_tok[b] + ([prev_ev2[par]] if prev_ev2[par] else [])
                            mm = None
                            for f in range(16):
                                mm = s.op("tensor", lambda e, f=f, b=b, tg=tg,
                                          ofs=ofs, par=par: e.matmul(
                                    psq[par][:, :],
                                    wqs[b][:, f * 512 + ofs:f * 512 + ofs + 128],
                                    hqF[:, f * D + tg * 512:f * D + (tg + 1) * 512],
                                    start=(f == 0), stop=(f == 15)),
                                    deps=(depf if f == 0 else []))
                            last_mm[b] = mm
                            tev = s.op("vector", lambda e, par=par, iq=iq: e.tensor_scalar(
                                evs[par][:], psq[par][:, :],
                                c.cols[:, 4 * iq + 2:4 * iq + 3], None, ALU.mult),
                                deps=[mm])
                            prev_ev2[par] = s.dma(
                                c.ch_out[par],
                                c.qkv_d[oc * 128:(oc + 1) * 128, tg * 512:(tg + 1) * 512],
                                evs[par][:], deps=[tev])
                a2_done = s.frontier()
                if c.debug:
                    s.dma(c.ch_misc, c.dbg["qkv0"][:, :],
                          c.qkv_d[:, 0:128], deps=a2_done)
        s.set_gate(a2_done)
        attn_scan_proj(c, in_ap, out_ap, gn_bc, gp_bc)


def attn_scan_proj(c, in_ap, out_ap, gn_bc, gp_bc):
    s, nc = c.s, _proxy(c)
    ip = c.mat_idx["proj"]
    import contextlib as _cl
    with _cl.ExitStack() as _st7:
        _e7 = _st7.enter_context
        z2F = _e7(nc.sbuf_tensor("z2F", [128, NT * D], BF16))
        S = _e7(nc.sbuf_tensor("S", [128, HEADS * 128], F32))
        Sb = _e7(nc.sbuf_tensor("Sb", [128, HEADS * 128], BF16))
        maskT = _e7(nc.sbuf_tensor("maskT", [128, HEADS * 128], F32))
        lamt_bc = _e7(nc.sbuf_tensor("lamt_bc", [128, HEADS * 128], BF16))
        lamr_bc = _e7(nc.sbuf_tensor("lamr_bc", [128, HEADS * 128], BF16))
        iotaG = _e7(nc.sbuf_tensor("iotaG", [128, 128], F32))
        indic = _e7(nc.sbuf_tensor("indic", [128, 128], F32))
        tG = s.op("gpsimd", lambda e: e.iota(iotaG[:], [[1, 128]], base=0,
                                             channel_multiplier=-1,
                                             allow_small_or_imprecise_dtypes=True))
        tGp = s.op("vector", lambda e: e.tensor_scalar(iotaG[:], iotaG[:], 0.0, None,
                                                       ALU.max), deps=[tG])
        tG2 = s.op("gpsimd", lambda e: e.iota(indic[:], [[1, 128]], base=0,
                                              channel_multiplier=-1,
                                              allow_small_or_imprecise_dtypes=True))
        tI = s.op("vector", lambda e: e.tensor_scalar(indic[:], indic[:], 0.0, None,
                                                      ALU.is_ge), deps=[tG2])
        mask_toks = []
        for h in range(HEADS):
            t1 = s.op("scalar", lambda e, h=h: e.activation(
                maskT[:, h * 128:(h + 1) * 128], iotaG[:], AF.Exp,
                scale=c.neg_s[:, h:h + 1]), deps=[tGp, c.tok_negs])
            t2 = s.op("vector", lambda e, h=h: e.tensor_mul(
                maskT[:, h * 128:(h + 1) * 128], maskT[:, h * 128:(h + 1) * 128],
                indic[:]), deps=[t1, tI])
            mask_toks.append(t2)
        tiorow = s.op("gpsimd", lambda e: e.iota(iotaG[:], [[1, 128]], base=0,
                                                 channel_multiplier=0,
                                                 allow_small_or_imprecise_dtypes=True),
                      deps=mask_toks)
        tiorev = s.op("gpsimd", lambda e: e.iota(indic[:], [[-1, 128]], base=128,
                                                 channel_multiplier=0,
                                                 allow_small_or_imprecise_dtypes=True),
                      deps=[tI] + mask_toks)
        lam_toks = []
        for h in range(HEADS):
            lam_toks.append(s.op("scalar", lambda e, h=h: e.activation(
                lamt_bc[:, h * 128:(h + 1) * 128], iotaG[:], AF.Exp,
                scale=c.neg_s[:, h:h + 1]), deps=[tiorow, c.tok_negs]))
            lam_toks.append(s.op("scalar", lambda e, h=h: e.activation(
                lamr_bc[:, h * 128:(h + 1) * 128], indic[:], AF.Exp,
                scale=c.neg_s[:, h:h + 1]), deps=[tiorev, c.tok_negs]))
        if c.debug:
            s.dma(c.ch_misc, c.dbg["mask"][:, :], maskT[:, :], deps=mask_toks)
            s.dma(c.ch_misc, c.dbg["lamt"][:, :], lamt_bc[:, :], deps=lam_toks)
            s.dma(c.ch_misc, c.dbg["lamr"][:, :], lamr_bc[:, :], deps=lam_toks)
            s.dma(c.ch_misc, c.dbg["lamT"][:, :], c.lamT[:, :], deps=lam_toks)
        tS0 = s.op("vector", lambda e: e.memset(S[:, :], 0.0))
        tSb = s.op("vector", lambda e: e.memset(Sb[:, :], 0.0), deps=[tS0])

        import contextlib
        with contextlib.ExitStack() as _st:
            _e = _st.enter_context
            slab0 = _e(nc.sbuf_tensor("slab0", [128, 48 * 128], BF16))
            slab1 = _e(nc.sbuf_tensor("slab1", [128, 48 * 128], BF16))
            qtb = _e(nc.sbuf_tensor("qtb", [128, HEADS * 128], BF16))
            ktb = _e(nc.sbuf_tensor("ktb", [128, 128], BF16))
            scm = _e(nc.sbuf_tensor("scm", [128, 128], BF16))
            ktT = _e(nc.sbuf_tensor("ktT", [128, 128], BF16))
            vT = _e(nc.sbuf_tensor("vT", [128, 128], BF16))
            outc = _e(nc.sbuf_tensor("outc", [128, D], F32))
            z1c = _e(nc.sbuf_tensor("z1c", [128, D], F32))
            z2b = _e(nc.sbuf_tensor("z2b", [128, D], BF16))
            dmp2 = _e(nc.sbuf_tensor("dmp2", [128, D], F32))
            ncol = _e(nc.sbuf_tensor("ncol", [128, 8], F32))
            ps_sc = _e(nc.psum_tensor("ps_sc", [128, 128], F32))
            ps_tt = _e(nc.psum_tensor("ps_tt", [128, 256], BF16))
            ps_tt2 = _e(nc.psum_tensor("ps_tt2", [128, 256], BF16))
            ps_o = _e(nc.psum_tensor("ps_o", [128, 128], F32))
            ps_kv = _e(nc.psum_tensor("ps_kv", [128, 128], F32))
            ps_trz0 = _e(nc.psum_tensor("ps_trz0", [128, 512], BF16))
            ps_trz1 = _e(nc.psum_tensor("ps_trz1", [128, 512], BF16))
            slabs = [slab0, slab1]
            prev_slab_rd = [None, None]
            prev_zev = [None, None]
            pt = {"t2": None, "t12": None, "t11": None, "t67": None, "t10": None}
            for i in range(NCH):
                b = i % 2
                tld = []
                for piece in range(4):
                    tld.append(s.dma(
                        c.ch_in[b],
                        slabs[b][:, piece * 1536:(piece + 1) * 1536].rearrange(
                            "p (a t) -> p a t", t=128),
                        c.qkv_d[piece * 1536:(piece + 1) * 1536,
                                i * 128:(i + 1) * 128].rearrange(
                            "(a p) t -> p a t", p=128),
                        deps=[prev_slab_rd[b]] if prev_slab_rd[b] else []))
                tcast = tld[0]
                tqt = s.op("vector", lambda e, b=b: e.tensor_tensor(
                    qtb[:, :], slabs[b][:, 0:HEADS * 128], lamt_bc[:, :], ALU.mult),
                    deps=tld + lam_toks + ([pt["t11"]] if pt["t11"] else []))
                last_k_rd = None
                for h in range(HEADS):
                    qs = slabs[b][:, h * 128:(h + 1) * 128]
                    ks = slabs[b][:, (16 + h) * 128:(17 + h) * 128]
                    vs = slabs[b][:, (32 + h) * 128:(33 + h) * 128]
                    t1 = s.op("tensor", lambda e, ks=ks, qs=qs: e.matmul(
                        ps_sc[:, :], ks, qs, start=True, stop=True),
                        deps=tld + ([pt["t2"]] if pt["t2"] else []))
                    t2 = s.op("vector", lambda e, h=h: e.tensor_mul(
                        scm[:], ps_sc[:, :], maskT[:, h * 128:(h + 1) * 128]),
                        deps=[t1] + mask_toks)
                    pt["t2"] = t2
                    t3 = s.op("vector", lambda e, h=h, b=b: e.tensor_tensor(
                        ktb[:], slabs[b][:, (16 + h) * 128:(17 + h) * 128],
                        lamr_bc[:, h * 128:(h + 1) * 128], ALU.mult),
                        deps=tld + lam_toks)
                    last_k_rd = t3
                    t4 = s.op("tensor", lambda e: e.transpose(ps_tt[:, 0:128], ktb[:],
                                                              c.id_b[:]),
                              deps=[t3, c.tok_id] + ([pt["t67"]] if pt["t67"] else []))
                    t5 = s.op("tensor", lambda e, vs=vs: e.transpose(
                        ps_tt2[:, 0:128], vs, c.id_b[:]),
                        deps=tld + [t4] + ([pt["t67"]] if pt["t67"] else []))
                    t6 = s.op("vector", lambda e: e.tensor_copy(ktT[:], ps_tt[:, 0:128]),
                              deps=[t4])
                    t7 = s.op("vector", lambda e: e.tensor_copy(vT[:], ps_tt2[:, 0:128]),
                              deps=[t5])
                    pt["t67"] = t7
                    if c.debug and i == 0 and h == 0:
                        s.dma(c.ch_misc, c.dbg["scm"][:, :], scm[:], deps=[t2])
                        s.dma(c.ch_misc, c.dbg["vT"][:, :], vT[:], deps=[t7])
                        s.dma(c.ch_misc, c.dbg["slab"][:, :], slabs[b][:, :], deps=tld)
                        t2 = Tok(c.ch_misc, s.chan_count[c.ch_misc])
                    t8 = s.op("tensor", lambda e: e.matmul(ps_o[:, :], scm[:], vT[:],
                                                           start=True, stop=False),
                              deps=[t2, t7] + ([pt["t12"]] if pt["t12"] else []))
                    t9 = s.op("tensor", lambda e, h=h: e.matmul(
                        ps_o[:, :], qtb[:, h * 128:(h + 1) * 128],
                        Sb[:, h * 128:(h + 1) * 128], start=False, stop=True),
                        deps=[tqt, t8, tSb])
                    t10 = s.op("tensor", lambda e: e.matmul(ps_kv[:, :], ktT[:], vT[:],
                                                            start=True, stop=True),
                               deps=[t6, t7, t9] + ([pt["t11"]] if pt["t11"] else []))
                    t11 = s.op("vector", lambda e, h=h: e.scalar_tensor_tensor(
                        S[:, h * 128:(h + 1) * 128], S[:, h * 128:(h + 1) * 128],
                        c.lamT[:, h:h + 1], ps_kv[:, :], ALU.mult, ALU.add),
                        deps=[t10, c.tok_lamT, tSb], sync_out=True)
                    pt["t11"] = t11
                    t12 = s.op("scalar", lambda e, h=h: e.activation(
                        outc[:, h * 128:(h + 1) * 128], ps_o[:, :], AF.Identity,
                        scale=SCALE_ATT),
                        deps=[t9] + ([pt.get("tz1")] if (h == 0 and pt.get("tz1")) else [])
                        + ([pt.get("outc_guard")] if (h == 0 and pt.get("outc_guard")) else []))
                    pt["t12"] = t12
                    if c.debug and i == 0 and h == 0:
                        tSb_tap = s.dma(c.ch_misc, c.dbg["Sb"][:, :], Sb[:, :], deps=[tSb])
                        s.dma(c.ch_misc, c.dbg["o0"][:, :], outc[:, 0:128], deps=[t12])
                        s.dma(c.ch_misc, c.dbg["qtb"][:, :], qtb[:, :], deps=[tqt])
                        pt["t12"] = Tok(c.ch_misc, s.chan_count[c.ch_misc])
                prev_slab_rd[b] = Tok("tensor", s.counts["tensor"])
                tSb = s.op("vector", lambda e: e.tensor_copy(Sb[:, :], S[:, :]),
                           deps=[pt["t11"], pt["t12"], pt["t10"] or pt["t11"]])
                if c.debug and i == 0:
                    tdo = s.dma(c.ch_misc, c.dbg["outc"][:, :], outc[:, :],
                                deps=[pt["t12"], pt["t11"]])
                    pt["outc_guard"] = tdo
                tn1 = s.op("scalar", lambda e: e.activation(
                    dmp2[:], outc[:], AF.Square, accum_out=ncol[:, 0:1]),
                    deps=[pt["t12"], pt["t11"]])
                tn2 = s.op("vector", lambda e: e.tensor_scalar(
                    ncol[:, 1:2], ncol[:, 0:1], 1.0 / D, 1e-6, ALU.mult, ALU.add),
                    deps=[tn1], sync_out=True)
                tn3 = s.op("scalar", lambda e: e.activation(ncol[:, 2:3], ncol[:, 1:2],
                                                            AF.Sqrt), deps=[tn2])
                tn4 = s.op("vector", lambda e: e.reciprocal(ncol[:, 3:4], ncol[:, 2:3]),
                           deps=[tn3], sync_out=True)
                tz1 = s.op("vector", lambda e: e.scalar_tensor_tensor(
                    z1c[:], outc[:], ncol[:, 3:4], gn_bc[:, :], ALU.mult, ALU.mult),
                    deps=[tn4])
                pt["tz1"] = tz1
                tm1 = s.op("scalar", lambda e: e.activation(
                    dmp2[:], z1c[:], AF.Square, accum_out=ncol[:, 4:5]), deps=[tz1])
                tm2 = s.op("vector", lambda e: e.tensor_scalar(
                    ncol[:, 5:6], ncol[:, 4:5], 1.0 / D, EPS_A, ALU.mult, ALU.add),
                    deps=[tm1], sync_out=True)
                tm3 = s.op("scalar", lambda e: e.activation(ncol[:, 6:7], ncol[:, 5:6],
                                                            AF.Sqrt), deps=[tm2])
                tm4 = s.op("vector", lambda e: e.reciprocal(ncol[:, 7:8], ncol[:, 6:7]),
                           deps=[tm3], sync_out=True)
                tz2 = s.op("vector", lambda e: e.scalar_tensor_tensor(
                    z1c[:], z1c[:], ncol[:, 7:8], gp_bc[:, :], ALU.mult, ALU.mult),
                    deps=[tm4])
                tzb = s.op("vector", lambda e: e.tensor_copy(z2b[:], z1c[:]), deps=[tz2])
                emit_transpose_quad(
                    c, z2b,
                    lambda g, i=i: z2F[:, :].rearrange("p (j t) -> p j t", t=D)[
                        :, 4 * g:4 * g + 4, i * 128:(i + 1) * 128],
                    [ps_trz0, ps_trz1], prev_zev, tzb, f"z2F{i}")
            a3_done = s.frontier()
            if c.debug:
                s.dma(c.ch_misc, c.dbg["z2F"][:, :], z2F[:, :], deps=a3_done)
        s.set_gate(a3_done)

        import contextlib as _cl
        with _cl.ExitStack() as _st8:
            _e8 = _st8.enter_context
            wp = _e8(nc.sbuf_tensor("wp", [128, 64 * 512], BF16))
            xsl2 = _e8(nc.sbuf_tensor("xsl2", [128, 2 * 512], F32))
            oa0 = _e8(nc.sbuf_tensor("oa0", [128, 512], F32))
            oa1 = _e8(nc.sbuf_tensor("oa1", [128, 512], F32))
            ps_p0 = _e8(nc.psum_tensor("ps_p0", [128, 512], F32))
            ps_p1 = _e8(nc.psum_tensor("ps_p1", [128, 512], F32))
            wp_toks = []
            for ob in range(4):
                for f in range(16):
                    wp_toks.append(s.dma(
                        c.ch_w[f % 2],
                        wp[:, (ob * 16 + f) * 512:(ob * 16 + f + 1) * 512],
                        c.tq["proj"][ob * 512:(ob + 1) * 512, f * 128:(f + 1) * 128],
                        deps=list(c.tern_tok["proj"]), transpose=True))
            psp, oas = [ps_p0, ps_p1], [oa0, oa1]
            prev_dma = [None, None]
            prev_ev = [None, None]
            prev_x = [None, None]
            for tc in range(NT):
                for ob in range(4):
                    par = (tc * 4 + ob) % 2
                    tx = s.dma(c.ch_in[par], xsl2[:, par * 512:(par + 1) * 512],
                               in_ap[tc * 128:(tc + 1) * 128, ob * 512:(ob + 1) * 512],
                               deps=[prev_x[par]] if prev_x[par] else [])
                    mm = None
                    for f in range(16):
                        mm = s.op("tensor", lambda e, f=f, tc=tc, ob=ob, par=par: e.matmul(
                            psp[par][:, :],
                            z2F[:, f * D + tc * 128:f * D + (tc + 1) * 128],
                            wp[:, (ob * 16 + f) * 512:(ob * 16 + f + 1) * 512],
                            start=(f == 0), stop=(f == 15)),
                            deps=((wp_toks + ([prev_ev[par]] if prev_ev[par] else []))
                                  if f == 0 else []))
                    tev = s.op("vector", lambda e, par=par, ip=ip: e.scalar_tensor_tensor(
                        oas[par][:], psp[par][:, :], c.cols[:, 4 * ip + 2:4 * ip + 3],
                        xsl2[:, par * 512:(par + 1) * 512], ALU.mult, ALU.add),
                        deps=[mm, tx] + ([prev_dma[par]] if prev_dma[par] else []))
                    tst = s.dma(c.ch_out[par],
                                out_ap[tc * 128:(tc + 1) * 128, ob * 512:(ob + 1) * 512],
                                oas[par][:], deps=[tev])
                    prev_ev[par] = tev
                    prev_dma[par] = tst
                    prev_x[par] = tev



def kernel(L0_attn_x, L0_attn_qkv_w, L0_attn_qkv_norm_w, L0_attn_proj_w,
           L0_attn_proj_norm_w, L0_attn_norm_w, L0_attn_s,
           L0_ffn_w_gate, L0_ffn_w_up, L0_ffn_w_down):
    x = np.ascontiguousarray(np.asarray(L0_attn_x, dtype=np.float32))
    base = {
        "w_qkv": np.ascontiguousarray(np.asarray(L0_attn_qkv_w, np.float32)),
        "w_proj": np.ascontiguousarray(np.asarray(L0_attn_proj_w, np.float32)),
        "w_gate": np.ascontiguousarray(np.asarray(L0_ffn_w_gate, np.float32)),
        "w_up": np.ascontiguousarray(np.asarray(L0_ffn_w_up, np.float32)),
        "w_down": np.ascontiguousarray(np.asarray(L0_ffn_w_down, np.float32)),
        "g_qkv": np.ascontiguousarray(np.asarray(L0_attn_qkv_norm_w, np.float32)),
        "g_proj": np.ascontiguousarray(np.asarray(L0_attn_proj_norm_w, np.float32)),
        "g_norm": np.ascontiguousarray(np.asarray(L0_attn_norm_w, np.float32)),
        "s_att": np.ascontiguousarray(np.asarray(L0_attn_s, np.float32)),
    }
    base.update(host_constants())
    nc = build_program()
    in_maps = []
    for b in range(8):
        m = dict(base)
        m["x_in"] = x[b]
        in_maps.append(m)
    res = run_bass_kernel_spmd(nc, in_maps, core_ids=list(range(8)))
    out = np.stack([np.asarray(r["x_out"]) for r in res.results], axis=0)
    return out.astype(np.float32)


# version 05
# speedup vs baseline: 1.0248x; 1.0248x over previous
import numpy as np
import concourse.bass as bass
import concourse.mybir as mybir
from concourse.bass_utils import run_bass_kernel_spmd

F32 = mybir.dt.float32
BF16 = mybir.dt.bfloat16
I32 = mybir.dt.int32
I8 = mybir.dt.int8
AF = mybir.ActivationFunctionType
ALU = mybir.AluOpType
AX = mybir.AxisListType.X

ENGINES = ("tensor", "vector", "scalar", "gpsimd", "sync")

D = 2048
N = 2048
NT = N // 128
HEADS = 16
CH = 128
NCH = N // CH
FI = 8192
FC = FI // 128
EPS_A = float(np.finfo(np.float32).eps)
SCALE_ATT = float(128.0) ** -0.5
LN_EPS = 1e-5
Q_EPS = 1e-5
C8192 = float(np.float32(1.0) / np.float32(np.sqrt(np.float32(8192.0))))


class Tok:
    __slots__ = ("res", "count", "sync")

    def __init__(self, res, count, sync=False):
        self.res = res
        self.count = count
        self.sync = sync


class Sched:
    def __init__(self, nc):
        self.nc = nc
        self.ops = {e: [] for e in ENGINES}
        self.counts = {e: 0 for e in ENGINES}
        self.watermark = {e: {} for e in ENGINES}
        self.channels = []
        self.chan_count = {}
        self.chan_engine = {}
        self._sems = {}
        self.gate = []

    def channel(self, name, engine="sync"):
        self.channels.append(name)
        self.chan_count[name] = 0
        self.chan_engine[name] = engine
        return name

    def set_gate(self, toks):
        self.gate = [t for t in toks if t is not None]

    def frontier(self):
        toks = [Tok(e, self.counts[e]) for e in ENGINES if self.counts[e] > 0]
        toks += [Tok(cn, self.chan_count[cn]) for cn in self.channels
                 if self.chan_count[cn] > 0]
        return toks

    def _wait_list(self, engine, deps):
        waits = []
        for t in list(deps) + self.gate:
            if t is None:
                continue
            if t.res == engine and not t.sync:
                continue
            wm = self.watermark[engine]
            if wm.get(t.res, 0) >= t.count:
                continue
            wm[t.res] = t.count
            waits.append((t.res, t.count))
        return waits

    def op(self, engine, fn, deps=(), sync_out=False):
        waits = self._wait_list(engine, deps)
        self.counts[engine] += 1
        tok = Tok(engine, self.counts[engine], sync=sync_out)
        self.ops[engine].append((fn, [(engine, 1)], waits))
        return tok

    def dma(self, chan, out, in_, deps=(), transpose=False):
        engine = self.chan_engine[chan]
        waits = self._wait_list(engine, deps)
        self.chan_count[chan] += 16
        tok = Tok(chan, self.chan_count[chan])

        def fn(eng, out=out, in_=in_, transpose=transpose):
            return eng.dma_start(out=out, in_=in_, transpose=transpose)

        self.ops[engine].append((fn, [(chan, 16)], waits))
        return tok

    def finalize(self):
        nc = self.nc
        import contextlib
        with contextlib.ExitStack() as stack:
            for res in list(ENGINES) + self.channels:
                self._sems[res] = stack.enter_context(nc.semaphore(f"s_{res}"))
            block = stack.enter_context(nc.Block())
            final_toks = [(cn, self.chan_count[cn]) for cn in self.channels
                          if self.chan_count[cn] > 0]

            def section(engine):
                def body(eng):
                    for fn, incs, waits in self.ops[engine]:
                        for res, cnt in waits:
                            eng.wait_ge(self._sems[res], cnt)
                        ins = fn(eng)
                        if ins is not None:
                            for res, amt in incs:
                                ins.then_inc(self._sems[res], amt)
                    if engine == "sync":
                        for res, cnt in final_toks:
                            eng.wait_ge(self._sems[res], cnt)
                return body

            for e in ENGINES:
                getattr(block, e)(section(e))


def _hadamard(n):
    h = np.array([[1.0]], dtype=np.float32)
    while h.shape[0] < n:
        h = np.block([[h, h], [h, -h]])
    return h.astype(np.float32)


def host_constants():
    return {"c_h128": _hadamard(128), "c_id": np.eye(128, dtype=np.float32)}


class Ctx:
    pass


def build_program(parts=("ffn1", "attn", "ffn2"), round_explicit=False, debug=False):
    nc = bass.Bass("TRN2", target_bir_lowering=False)
    s = Sched(nc)
    c = Ctx()
    c.nc, c.s = nc, s
    c.round_explicit = round_explicit

    x_in = nc.dram_tensor("x_in", [N, D], F32, kind="ExternalInput").ap()
    x_out = nc.dram_tensor("x_out", [N, D], F32, kind="ExternalOutput").ap()
    c.w_nat = {
        "qkv": nc.dram_tensor("w_qkv", [3 * D, D], F32, kind="ExternalInput").ap(),
        "proj": nc.dram_tensor("w_proj", [D, D], F32, kind="ExternalInput").ap(),
        "gate": nc.dram_tensor("w_gate", [FI, D], F32, kind="ExternalInput").ap(),
        "up": nc.dram_tensor("w_up", [FI, D], F32, kind="ExternalInput").ap(),
        "down": nc.dram_tensor("w_down", [D, FI], F32, kind="ExternalInput").ap(),
    }
    c.g_qkv = nc.dram_tensor("g_qkv", [D], F32, kind="ExternalInput").ap()
    c.g_proj = nc.dram_tensor("g_proj", [D], F32, kind="ExternalInput").ap()
    c.g_norm = nc.dram_tensor("g_norm", [D], F32, kind="ExternalInput").ap()
    c.s_att = nc.dram_tensor("s_att", [HEADS], F32, kind="ExternalInput").ap()
    c.c_h128_d = nc.dram_tensor("c_h128", [128, 128], F32, kind="ExternalInput").ap()
    c.c_id_d = nc.dram_tensor("c_id", [128, 128], F32, kind="ExternalInput").ap()

    c.tq = {}
    for name, (o, k) in [("qkv", (3 * D, D)), ("proj", (D, D)), ("gate", (FI, D)),
                         ("up", (FI, D)), ("down", (D, FI))]:
        c.tq[name] = nc.dram_tensor(f"tq_{name}", [o, k], BF16, kind="Internal").ap()
    xb1 = nc.dram_tensor("xb1", [N, D], F32, kind="Internal").ap()
    xb2 = nc.dram_tensor("xb2", [N, D], F32, kind="Internal").ap()
    c.inter_d = nc.dram_tensor("inter_d", [FI, N], BF16, kind="Internal").ap()
    c.qkv_d = nc.dram_tensor("qkv_d", [3 * D, N], BF16, kind="Internal").ap()
    c.row_d = nc.dram_tensor("row_d", [16, 128], F32, kind="Internal").ap()

    c.debug = debug
    c.dbg = {}
    if debug:
        c.dbg["scal"] = nc.dram_tensor("d_scal", [1, 64], F32, kind="ExternalOutput").ap()
        c.dbg["a1"] = nc.dram_tensor("d_a1", [128, NT], F32, kind="ExternalOutput").ap()
        c.dbg["n1F"] = nc.dram_tensor("d_n1F", [128, NT * D], BF16, kind="ExternalOutput").ap()
        c.dbg["inter"] = nc.dram_tensor("d_inter", [FI, 128], BF16, kind="ExternalOutput").ap()
        c.dbg["bf"] = nc.dram_tensor("d_bf", [128, FI], BF16, kind="ExternalOutput").ap()
        c.dbg["zq"] = nc.dram_tensor("d_zq", [128, FI], F32, kind="ExternalOutput").ap()
        c.dbg["qc"] = nc.dram_tensor("d_qc", [128, 8], F32, kind="ExternalOutput").ap()
        c.dbg["n2"] = nc.dram_tensor("d_n2", [128, FC * 512], BF16, kind="ExternalOutput").ap()
        c.dbg["beta"] = nc.dram_tensor("d_beta", [128, NT], F32, kind="ExternalOutput").ap()
        c.dbg["wg0"] = nc.dram_tensor("d_wg0", [128, 16 * 512], BF16, kind="ExternalOutput").ap()
        c.dbg["bcga"] = nc.dram_tensor("d_bcga", [128, N], F32, kind="ExternalOutput").ap()
        c.dbg["gate0"] = nc.dram_tensor("d_gate0", [128, 512], F32, kind="ExternalOutput").ap()
        c.dbg["a1row"] = nc.dram_tensor("d_a1row", [1, N], F32, kind="ExternalOutput").ap()
        c.dbg["hqF"] = nc.dram_tensor("d_hqF", [128, NT * D], BF16, kind="ExternalOutput").ap()
        c.dbg["qkv0"] = nc.dram_tensor("d_qkv0", [3 * D, 128], BF16, kind="ExternalOutput").ap()
        c.dbg["mask"] = nc.dram_tensor("d_mask", [128, HEADS * 128], F32, kind="ExternalOutput").ap()
        c.dbg["lamt"] = nc.dram_tensor("d_lamt", [128, HEADS * 128], BF16, kind="ExternalOutput").ap()
        c.dbg["lamr"] = nc.dram_tensor("d_lamr", [128, HEADS * 128], BF16, kind="ExternalOutput").ap()
        c.dbg["lamT"] = nc.dram_tensor("d_lamT", [128, HEADS], F32, kind="ExternalOutput").ap()
        c.dbg["outc"] = nc.dram_tensor("d_outc", [128, D], F32, kind="ExternalOutput").ap()
        c.dbg["z2F"] = nc.dram_tensor("d_z2F", [128, NT * D], BF16, kind="ExternalOutput").ap()
        c.dbg["scm"] = nc.dram_tensor("d_scm", [128, 128], BF16, kind="ExternalOutput").ap()
        c.dbg["vT"] = nc.dram_tensor("d_vT", [128, 128], BF16, kind="ExternalOutput").ap()
        c.dbg["slab"] = nc.dram_tensor("d_slab", [128, 48 * 128], BF16, kind="ExternalOutput").ap()
        c.dbg["Sb"] = nc.dram_tensor("d_Sb", [128, HEADS * 128], BF16, kind="ExternalOutput").ap()
        c.dbg["o0"] = nc.dram_tensor("d_o0", [128, 128], F32, kind="ExternalOutput").ap()
        c.dbg["qtb"] = nc.dram_tensor("d_qtb", [128, HEADS * 128], BF16, kind="ExternalOutput").ap()
    c.ch_in = [s.channel("in0", "sync"), s.channel("in1", "scalar")]
    c.ch_w = [s.channel("w0", "scalar"), s.channel("w1", "scalar")]
    c.ch_out = [s.channel("out0", "sync"), s.channel("out1", "scalar")]
    c.ch_misc = s.channel("misc")
    c.ps_guard = None

    import contextlib
    with contextlib.ExitStack() as stack:
        sb = lambda name, shape, dt: stack.enter_context(nc.sbuf_tensor(name, shape, dt))
        c.ones1 = sb("ones1", [1, 128], F32)
        c.onescol = sb("onescol", [128, 1], F32)
        c.id_f = sb("id_f", [128, 128], F32)
        c.id_b = sb("id_b", [128, 128], BF16)
        c.h128b = sb("h128b", [128, 128], BF16)
        c.scal = sb("scal", [1, 64], F32)
        c.cols = sb("cols", [128, 64], F32)
        c.neg_s = sb("neg_s", [128, HEADS], F32)
        c.lamT = sb("lamT", [128, HEADS], F32)
        c.a1cols = sb("a1cols", [128, NT], F32)
        c.a2cols = sb("a2cols", [128, NT], F32)
        c.betacols = sb("betacols", [128, NT], F32)
        c.ps_small = stack.enter_context(nc.psum_tensor("ps_small", [128, 512], F32))

        setup_constants(c)
        setup_quantize_weights(c)
        s.set_gate(s.frontier())

        io = {"ffn1": (x_in, xb1), "attn": (xb1, xb2), "ffn2": (xb2, x_out)}
        for pi, p in enumerate(parts):
            a, b = io[p]
            if pi == 0:
                a = x_in
            if pi == len(parts) - 1:
                b = x_out
            if p == "attn":
                attn_sublayer(c, a, b)
            else:
                ffn_sublayer(c, a, b)
            s.set_gate(s.frontier())
        s.finalize()
    return nc



def bcast_col(c, src11, dst_col, deps, scale=None):
    s = c.s
    tm = s.op("tensor", lambda e: e.matmul(c.ps_small[:, 0:1], c.ones1[:, :], src11,
                                           start=True, stop=True),
              deps=list(deps) + [c.tok_ones] + ([c.ps_guard] if c.ps_guard else []))
    if scale is None:
        t2 = s.op("vector", lambda e: e.tensor_copy(dst_col, c.ps_small[:, 0:1]),
                  deps=[tm], sync_out=True)
    else:
        t2 = s.op("vector", lambda e: e.tensor_scalar(dst_col, c.ps_small[:, 0:1],
                                                      scale, None, ALU.mult), deps=[tm], sync_out=True)
    c.ps_guard = t2
    return t2


def setup_constants(c):
    s, nc = c.s, c.nc
    c.tok_ones = s.op("vector", lambda e: e.memset(c.ones1[:, :], 1.0))
    c.tok_onesc = s.op("vector", lambda e: e.memset(c.onescol[:, :], 1.0))
    t_id = s.dma(c.ch_misc, c.id_f[:], c.c_id_d[:, :])
    c.tok_id = s.op("vector", lambda e: e.tensor_copy(c.id_b[:], c.id_f[:]), deps=[t_id])
    c.tok_idf = t_id
    with nc.sbuf_tensor("h128f", [128, 128], F32) as h128f:
        th = s.dma(c.ch_misc, h128f[:], c.c_h128_d[:, :])
        c.tok_h = s.op("vector", lambda e: e.tensor_copy(c.h128b[:], h128f[:]), deps=[th])
    s.set_gate(s.frontier())
    with nc.sbuf_tensor("srow", [1, HEADS], F32) as srow:
        ts0 = s.dma(c.ch_misc, srow[:, :], c.s_att[:].rearrange("(o h) -> o h", o=1))
        tm = s.op("tensor", lambda e: e.matmul(c.ps_small[:, 0:HEADS], c.ones1[:, :],
                                               srow[:, :], start=True, stop=True),
                  deps=[ts0, c.tok_ones, c.tok_h])
        tneg = s.op("vector", lambda e: e.tensor_scalar(c.neg_s[:, :],
                                                        c.ps_small[:, 0:HEADS],
                                                        -1.0, None, ALU.mult), deps=[tm])
        tl = s.op("scalar", lambda e: e.activation(c.lamT[:, :], c.ps_small[:, 0:HEADS],
                                                   AF.Exp, scale=-128.0), deps=[tm])
        c.tok_negs, c.tok_lamT = tneg, tl
        c.ps_guard = tl
    s.set_gate(s.frontier())


def setup_quantize_weights(c):
    s, nc = c.s, c.nc
    mats = ["gate", "up", "down", "qkv", "proj"]
    c.mat_idx = {m: i for i, m in enumerate(mats)}
    c.tern_tok = {}
    import contextlib
    with contextlib.ExitStack() as _st:
        _e = _st.enter_context
        wt0 = _e(nc.sbuf_tensor("wt0", [128, 8192], F32))
        wt1 = _e(nc.sbuf_tensor("wt1", [128, 8192], F32))
        dump = _e(nc.sbuf_tensor("dumpw", [128, 8192], F32))
        cmp1 = _e(nc.sbuf_tensor("cmp1", [128, 8192], F32))
        tnb0 = _e(nc.sbuf_tensor("tnb0", [128, 8192], BF16))
        tnb1 = _e(nc.sbuf_tensor("tnb1", [128, 8192], BF16))
        acc = _e(nc.sbuf_tensor("accw", [128, 16], F32))
        wts, tnbs = [wt0, wt1], [tnb0, tnb1]
        for m in mats:
            i = c.mat_idx[m]
            w = c.w_nat[m]
            O, K = w.tensor.shape
            nelem = O * K
            cols = min(K, 8192)
            ntr, ntc = O // 128, K // cols
            t_acc0 = s.op("vector", lambda e, i=i: e.memset(acc[:, i:i + 1], 0.0))
            last = [t_acc0]
            prev_use = [None, None]
            for r in range(ntr):
                for cc in range(ntc):
                    b = (r * ntc + cc) % 2
                    tl = s.dma(c.ch_in[b], wts[b][:, 0:cols],
                               w[r * 128:(r + 1) * 128, cc * cols:(cc + 1) * cols],
                               deps=[prev_use[b]] if prev_use[b] else [])
                    ta = s.op("scalar", lambda e, b=b, i=i, cols=cols: e.activation(
                        dump[:, 0:cols], wts[b][:, 0:cols], AF.Abs,
                        accum_out=acc[:, 8 + i:9 + i]), deps=[tl] + last)
                    tacc = s.op("vector", lambda e, i=i: e.tensor_tensor(
                        acc[:, i:i + 1], acc[:, i:i + 1], acc[:, 8 + i:9 + i], ALU.add),
                        deps=[ta], sync_out=True)
                    prev_use[b] = ta
                    last = [tacc]
            tm = s.op("tensor", lambda e, i=i: e.matmul(
                c.ps_small[0:1, 0:1], acc[:, i:i + 1], c.onescol[:, :],
                start=True, stop=True),
                deps=last + [c.tok_onesc] + ([c.ps_guard] if c.ps_guard else []))
            tv = s.op("vector", lambda e, i=i, n=nelem: e.tensor_scalar(
                c.scal[0:1, 2 * i:2 * i + 1], c.ps_small[0:1, 0:1],
                1.0 / n, Q_EPS, ALU.mult, ALU.max), deps=[tm], sync_out=True)
            c.ps_guard = tv
            tt = s.op("vector", lambda e, i=i: e.tensor_scalar(
                c.scal[0:1, 2 * i + 1:2 * i + 2], c.scal[0:1, 2 * i:2 * i + 1],
                0.5, None, ALU.mult), deps=[tv], sync_out=True)
            tc1 = bcast_col(c, c.scal[0:1, 2 * i + 1:2 * i + 2],
                            c.cols[:, 4 * i:4 * i + 1], [tt])
            tc2 = s.op("vector", lambda e, i=i: e.tensor_scalar(
                c.cols[:, 4 * i + 1:4 * i + 2], c.cols[:, 4 * i:4 * i + 1],
                -1.0, None, ALU.mult), deps=[tc1], sync_out=True)
            tc3 = bcast_col(c, c.scal[0:1, 2 * i:2 * i + 1],
                            c.cols[:, 4 * i + 3:4 * i + 4], [tv])
            tc4 = s.op("vector", lambda e, i=i: e.tensor_scalar(
                c.cols[:, 4 * i + 2:4 * i + 3], c.cols[:, 4 * i + 3:4 * i + 4],
                -1.0, None, ALU.mult), deps=[tc3], sync_out=True)
            prev_use = [None, None]
            prev_st = [None, None]
            tern_last = [None, None]
            for r in range(ntr):
                for cc in range(ntc):
                    b = (r * ntc + cc) % 2
                    tl = s.dma(c.ch_in[b], wts[b][:, 0:cols],
                               w[r * 128:(r + 1) * 128, cc * cols:(cc + 1) * cols],
                               deps=([prev_use[b]] if prev_use[b] else [tc2, tc4]))
                    tg = s.op("vector", lambda e, b=b, i=i, cols=cols: e.tensor_scalar(
                        cmp1[:, 0:cols], wts[b][:, 0:cols],
                        c.cols[:, 4 * i:4 * i + 1], None, ALU.is_ge),
                        deps=[tl, tc2, tc4] + ([prev_st[b]] if prev_st[b] else []))
                    tn = s.op("vector", lambda e, b=b, i=i, cols=cols: e.scalar_tensor_tensor(
                        tnbs[b][:, 0:cols], wts[b][:, 0:cols],
                        c.cols[:, 4 * i + 1:4 * i + 2], cmp1[:, 0:cols],
                        ALU.is_le, ALU.subtract), deps=[tg])
                    prev_use[b] = tn
                    tst = s.dma(c.ch_out[b],
                                c.tq[m][r * 128:(r + 1) * 128, cc * cols:(cc + 1) * cols],
                                tnbs[b][:, 0:cols], deps=[tn])
                    prev_st[b] = tst
                    tern_last[b] = tst
            c.tern_tok[m] = [t for t in tern_last if t is not None] + [tc2, tc4]



def emit_ln(c, x_tile, y_tile, dump, colbuf, deps, eps=LN_EPS, war=()):
    s = c.s
    cs, cq, cm, cmsq, cvar, cstd, crstd, cneg = (colbuf[:, i:i + 1] for i in range(8))
    t1 = s.op("vector", lambda e: e.reduce_sum(cs, x_tile, AX), deps=deps, sync_out=True)
    t2 = s.op("scalar", lambda e: e.activation(dump, x_tile, AF.Square, accum_out=cq),
              deps=deps)
    t3 = s.op("vector", lambda e: e.tensor_scalar(cm, cs, 1.0 / D, None, ALU.mult),
              deps=[t1], sync_out=True)
    t4 = s.op("vector", lambda e: e.tensor_scalar(cmsq, cq, 1.0 / D, None, ALU.mult),
              deps=[t2], sync_out=True)
    t5 = s.op("vector", lambda e: e.tensor_mul(cvar, cm, cm), deps=[t3], sync_out=True)
    t6 = s.op("vector", lambda e: e.tensor_sub(cvar, cmsq, cvar), deps=[t4, t5], sync_out=True)
    t7 = s.op("vector", lambda e: e.tensor_scalar(cvar, cvar, eps, None, ALU.add),
              deps=[t6], sync_out=True)
    t8 = s.op("scalar", lambda e: e.activation(cstd, cvar, AF.Sqrt), deps=[t7])
    t9 = s.op("vector", lambda e: e.reciprocal(crstd, cstd), deps=[t8], sync_out=True)
    t10 = s.op("vector", lambda e: e.tensor_scalar(cneg, cm, crstd, -1.0,
                                                   ALU.mult, ALU.mult), deps=[t9], sync_out=True)
    t11 = s.op("scalar", lambda e: e.activation(y_tile, x_tile, AF.Identity,
                                                bias=cneg, scale=crstd),
               deps=[t10] + list(war))
    return t11, [t1, t11]


def emit_quant(c, y_tile, n_bf_tile, i8_tile, colbuf, alpha_dst, deps, war=()):
    s = c.s
    cmx, cmxc, crec, cs1 = (colbuf[:, i:i + 1] for i in range(4))
    t1 = s.op("vector", lambda e: e.reduce_max(cmx, y_tile, AX,
                                               apply_absolute_value=True), deps=deps, sync_out=True)
    t2 = s.op("vector", lambda e: e.tensor_scalar(cmxc, cmx, Q_EPS, None, ALU.max),
              deps=[t1], sync_out=True)
    t3 = s.op("vector", lambda e: e.reciprocal(crec, cmxc), deps=[t2], sync_out=True)
    t4 = s.op("vector", lambda e: e.tensor_scalar(cs1, crec, 128.0, None, ALU.mult),
              deps=[t3], sync_out=True)
    ta = s.op("vector", lambda e: e.tensor_scalar(alpha_dst, cmxc, 1.0 / 128.0,
                                                  None, ALU.mult), deps=[t2], sync_out=True)
    t5 = s.op("vector", lambda e: e.tensor_scalar(i8_tile, y_tile, cs1, None, ALU.mult),
              deps=[t4])
    t6 = s.op("vector", lambda e: e.tensor_copy(n_bf_tile, i8_tile),
              deps=[t5] + list(war))
    return t6, ta


def emit_transpose_quad(c, src_bf, dst_view_fn, ps_pair, prev_ev, dep, tag):
    s = c.s
    out_toks = []
    for g in range(4):
        pb = ps_pair[g % 2]
        trs = []
        for j2 in range(4):
            j = g * 4 + j2
            trs.append(s.op("tensor", lambda e, j=j, j2=j2, pb=pb: e.transpose(
                pb[:, j2 * 128:(j2 + 1) * 128], src_bf[:, j * 128:(j + 1) * 128],
                c.id_b[:]),
                deps=[dep, c.tok_id] + ([prev_ev[g % 2]] if prev_ev[g % 2] else [])))
        tev = s.op("vector", lambda e, g=g, pb=pb: e.tensor_copy(
            dst_view_fn(g), pb[:, :].rearrange("p (j t) -> p j t", t=128)), deps=trs)
        prev_ev[g % 2] = tev
        out_toks.append(tev)
    return out_toks


def emit_row_from_cols(c, cols_ap, row_ap, nchunk, deps):
    s, nc = c.s, _proxy(c)
    ttr = s.op("tensor", lambda e: e.transpose(
        c.ps_small[0:nchunk, 0:128], cols_ap, c.id_f[:]),
        deps=list(deps) + [c.tok_idf] + ([c.ps_guard] if c.ps_guard else []))
    with nc.sbuf_tensor("rowT", [nchunk, 128], F32) as rowT:
        tev = s.op("vector", lambda e: e.tensor_copy(rowT[:, :],
                                                     c.ps_small[0:nchunk, 0:128]),
                   deps=[ttr])
        c.ps_guard = tev
        tst = s.dma(c.ch_misc, c.row_d[0:nchunk, :], rowT[:, :], deps=[tev])
        tld = s.dma(c.ch_misc, row_ap,
                    c.row_d[0:nchunk, :].rearrange("(o i) p -> o (i p)", o=1), deps=[tst])
    s.set_gate(s.frontier())
    return tld


def emit_bc_rows(c, row_ap, dst, scale_col, deps, n=N):
    s = c.s
    toks = []
    for seg in range(n // 512):
        tm = s.op("tensor", lambda e, seg=seg: e.matmul(
            c.ps_small[:, 0:512], c.ones1[:, :], row_ap[0:1, seg * 512:(seg + 1) * 512],
            start=True, stop=True),
            deps=list(deps) + [c.tok_ones] + ([c.ps_guard] if c.ps_guard else []))
        if scale_col is None:
            tev = s.op("vector", lambda e, seg=seg: e.tensor_copy(
                dst[:, seg * 512:(seg + 1) * 512], c.ps_small[:, 0:512]), deps=[tm])
        else:
            tev = s.op("vector", lambda e, seg=seg: e.tensor_scalar(
                dst[:, seg * 512:(seg + 1) * 512], c.ps_small[:, 0:512],
                scale_col, None, ALU.mult), deps=[tm])
        c.ps_guard = tev
        toks.append(tev)
    return toks





class _NCProxy:
    _n = [0]

    def __init__(self, nc):
        self._nc = nc
        _NCProxy._n[0] += 1
        self._tag = "_u%d" % _NCProxy._n[0]

    def sbuf_tensor(self, name, shape, dt):
        return self._nc.sbuf_tensor(name + self._tag, shape, dt)

    def psum_tensor(self, name, shape, dt):
        return self._nc.psum_tensor(name + self._tag, shape, dt)


def _proxy(c):
    return _NCProxy(c.nc)

def ffn_sublayer(c, in_ap, out_ap):
    s, nc = c.s, _proxy(c)
    ig, iu = c.mat_idx["gate"], c.mat_idx["up"]
    import contextlib as _cl
    with _cl.ExitStack() as _st0:
        _e0 = _st0.enter_context
        n1F = _e0(nc.sbuf_tensor("n1F", [128, NT * D], BF16))
        bcga = _e0(nc.sbuf_tensor("bcga", [128, N], F32))
        bcua = _e0(nc.sbuf_tensor("bcua", [128, N], F32))
        c.a1row = _e0(nc.sbuf_tensor("a1row", [1, N], F32))
        import contextlib as _cl
        with _cl.ExitStack() as _st1:
            _e1 = _st1.enter_context
            xt0 = _e1(nc.sbuf_tensor("xt0", [128, D], F32))
            xt1 = _e1(nc.sbuf_tensor("xt1", [128, D], F32))
            y1t = _e1(nc.sbuf_tensor("y1", [128, D], F32))
            dmp = _e1(nc.sbuf_tensor("dmp", [128, D], F32))
            i8t = _e1(nc.sbuf_tensor("i8t", [128, D], I8))
            nbt = _e1(nc.sbuf_tensor("nbt", [128, D], BF16))
            lncol = _e1(nc.sbuf_tensor("lncol", [128, 16], F32))
            ps_tr0 = _e1(nc.psum_tensor("ps_tr0", [128, 512], BF16))
            ps_tr1 = _e1(nc.psum_tensor("ps_tr1", [128, 512], BF16))
            xts = [xt0, xt1]
            prev_rd = [None, None]
            prev_rd2 = [None, None]
            prev_ev = [None, None]
            prev_q = [None]
            for i in range(NT):
                b = i % 2
                tl = s.dma(c.ch_in[b], xts[b][:], in_ap[i * 128:(i + 1) * 128, :],
                           deps=[prev_rd[b], prev_rd2[b]] if prev_rd[b] else [])
                ty, readers = emit_ln(c, xts[b][:], y1t[:], dmp[:], lncol, [tl],
                                      war=[prev_q[0]] if prev_q[0] else [])
                prev_rd[b] = ty
                prev_rd2[b] = readers[0]
                tq_, ta = emit_quant(c, y1t[:], nbt[:], i8t[:], lncol[:, 8:12],
                                     c.a1cols[:, i:i + 1], [ty],
                                     war=[t for t in prev_ev if t])
                prev_q[0] = tq_
                emit_transpose_quad(
                    c, nbt,
                    lambda g, i=i: n1F[:, :].rearrange("p (j t) -> p j t", t=D)[
                        :, 4 * g:4 * g + 4, i * 128:(i + 1) * 128],
                    [ps_tr0, ps_tr1], prev_ev, tq_, f"n1F{i}")
            p1_done = s.frontier()
        s.set_gate(p1_done)
        if c.debug and not getattr(c, "dbg_done", False):
            s.dma(c.ch_misc, c.dbg["a1"][:, :], c.a1cols[:, :], deps=p1_done)
            s.dma(c.ch_misc, c.dbg["n1F"][:, :], n1F[:, :], deps=p1_done)

        tld = emit_row_from_cols(c, c.a1cols[:, :], c.a1row[:, :], NT, p1_done)
        if c.debug and not getattr(c, "dbg_done", False):
            s.dma(c.ch_misc, c.dbg["a1row"][:, :], c.a1row[:, :], deps=[tld])
        bc1 = emit_bc_rows(c, c.a1row, bcga, c.cols[:, 4 * ig + 2:4 * ig + 3], [tld])
        bc2 = emit_bc_rows(c, c.a1row, bcua, c.cols[:, 4 * iu + 2:4 * iu + 3], [tld])
        bc_toks = bc1 + bc2

        import contextlib as _cl
        with _cl.ExitStack() as _st2:
            _e2 = _st2.enter_context
            wg0 = _e2(nc.sbuf_tensor("wg0", [128, 16 * 512], BF16))
            wg1 = _e2(nc.sbuf_tensor("wg1", [128, 16 * 512], BF16))
            wu0 = _e2(nc.sbuf_tensor("wu0", [128, 16 * 512], BF16))
            wu1 = _e2(nc.sbuf_tensor("wu1", [128, 16 * 512], BF16))
            ev1 = [_e2(nc.sbuf_tensor("ev1a", [128, 512], F32)),
                   _e2(nc.sbuf_tensor("ev1b", [128, 512], F32))]
            ev2 = [_e2(nc.sbuf_tensor("ev2a", [128, 512], F32)),
                   _e2(nc.sbuf_tensor("ev2b", [128, 512], F32))]
            ev3 = [_e2(nc.sbuf_tensor("ev3a", [128, 512], F32)),
                   _e2(nc.sbuf_tensor("ev3b", [128, 512], F32))]
            ist0 = _e2(nc.sbuf_tensor("ist0", [128, N], BF16))
            ist1 = _e2(nc.sbuf_tensor("ist1", [128, N], BF16))
            ps_g0 = _e2(nc.psum_tensor("ps_g0", [128, 512], F32))
            ps_g1 = _e2(nc.psum_tensor("ps_g1", [128, 512], F32))
            ps_u0 = _e2(nc.psum_tensor("ps_u0", [128, 512], F32))
            ps_u1 = _e2(nc.psum_tensor("ps_u1", [128, 512], F32))
            wgs, wus = [wg0, wg1], [wu0, wu1]
            psg, psu = [ps_g0, ps_g1], [ps_u0, ps_u1]
            ists = [ist0, ist1]

            def load_group(og, b, deps):
                toks = []
                for f in range(16):
                    toks.append(s.dma(c.ch_w[0], wgs[b][:, f * 512:(f + 1) * 512],
                                      c.tq["gate"][og * 512:(og + 1) * 512,
                                                   f * 128:(f + 1) * 128],
                                      deps=deps, transpose=True))
                    toks.append(s.dma(c.ch_w[1], wus[b][:, f * 512:(f + 1) * 512],
                                      c.tq["up"][og * 512:(og + 1) * 512,
                                                 f * 128:(f + 1) * 128],
                                      deps=deps, transpose=True))
                return toks

            if c.debug and not getattr(c, "dbg_done", False):
                pass
            wslab_tok = [None, None]
            last_mm_for_slab = [None, None]
            prev_evac = {0: None, 1: None}
            prev_ist_dma = [None, None]
            wdeps = list(c.tern_tok["gate"]) + list(c.tern_tok["up"])
            wslab_tok[0] = load_group(0, 0, wdeps)
            if c.debug and not getattr(c, "dbg_done", False):
                s.dma(c.ch_misc, c.dbg["wg0"][:, :], wg0[:, :], deps=wslab_tok[0])
                s.dma(c.ch_misc, c.dbg["bcga"][:, :], bcga[:, :], deps=bc_toks)
            for og in range(16):
                b = og % 2
                if og + 1 < 16:
                    nb = (og + 1) % 2
                    wslab_tok[nb] = load_group(
                        og + 1, nb,
                        ([last_mm_for_slab[nb]] if last_mm_for_slab[nb] else []) + wdeps)
                for oc4 in range(4):
                    oc = og * 4 + oc4
                    ofs = oc4 * 128
                    for tg in range(4):
                        par = (oc4 * 4 + tg) % 2
                        depf = wslab_tok[b] + ([prev_evac[par]] if prev_evac[par] else [])
                        mm_g = mm_u = None
                        for f in range(16):
                            mm_g = s.op("tensor", lambda e, f=f, b=b, tg=tg, par=par,
                                        ofs=ofs: e.matmul(
                                psg[par][:, :],
                                wgs[b][:, f * 512 + ofs:f * 512 + ofs + 128],
                                n1F[:, f * D + tg * 512:f * D + (tg + 1) * 512],
                                start=(f == 0), stop=(f == 15)),
                                deps=(depf if f == 0 else []))
                        for f in range(16):
                            mm_u = s.op("tensor", lambda e, f=f, b=b, tg=tg, par=par,
                                        ofs=ofs: e.matmul(
                                psu[par][:, :],
                                wus[b][:, f * 512 + ofs:f * 512 + ofs + 128],
                                n1F[:, f * D + tg * 512:f * D + (tg + 1) * 512],
                                start=(f == 0), stop=(f == 15)),
                                deps=(depf if f == 0 else []))
                        last_mm_for_slab[b] = mm_u
                        te1 = s.op("vector", lambda e, par=par, tg=tg: e.tensor_mul(
                            ev1[par][:], psg[par][:, :], bcga[:, tg * 512:(tg + 1) * 512]),
                            deps=[mm_g] + bc_toks)
                        if c.debug and og == 0 and oc4 == 0 and tg == 0 and not getattr(c, "dbg_done", False):
                            s.dma(c.ch_misc, c.dbg["gate0"][:, :], ev1[par][:], deps=[te1])
                            te1 = Tok(c.ch_misc, s.chan_count[c.ch_misc])
                        te2 = s.op("scalar", lambda e, par=par: e.activation(
                            ev2[par][:], ev1[par][:], AF.Silu), deps=[te1])
                        te3 = s.op("vector", lambda e, par=par, tg=tg: e.tensor_mul(
                            ev3[par][:], psu[par][:, :], bcua[:, tg * 512:(tg + 1) * 512]),
                            deps=[mm_u] + bc_toks)
                        te4 = s.op("vector", lambda e, oc=oc, tg=tg, par=par: e.tensor_mul(
                            ists[oc % 2][:, tg * 512:(tg + 1) * 512], ev2[par][:],
                            ev3[par][:]),
                            deps=[te2, te3] + ([prev_ist_dma[oc % 2]]
                                               if (prev_ist_dma[oc % 2] and tg == 0) else []))
                        prev_evac[par] = te4
                    tso = s.dma(c.ch_out[oc % 2], c.inter_d[oc * 128:(oc + 1) * 128, :],
                                ists[oc % 2][:], deps=[prev_evac[0], prev_evac[1]])
                    prev_ist_dma[oc % 2] = tso
            p2_done = s.frontier()
        if c.debug and not getattr(c, "dbg_done", False):
            s.dma(c.ch_misc, c.dbg["scal"][:, :], c.scal[:, :], deps=p2_done)
            s.dma(c.ch_misc, c.dbg["inter"][:, :], c.inter_d[:, 0:128], deps=p2_done)
    s.set_gate(p2_done)
    ffn_p34(c, in_ap, out_ap)
    c.dbg_done = True


def ffn_p34(c, in_ap, out_ap):
    s, nc = c.s, _proxy(c)
    idn = c.mat_idx["down"]
    import contextlib as _cl
    with _cl.ExitStack() as _st3:
        _e3 = _st3.enter_context
        n2sb = _e3(nc.sbuf_tensor("n2sb", [128, FC * 512], BF16))
        interb = _e3(nc.sbuf_tensor("interb", [128, FI], BF16))
        zq = _e3(nc.sbuf_tensor("zq", [128, FI], F32))
        mscr = _e3(nc.sbuf_tensor("mscr", [128, 4096], F32))
        mscr2 = _e3(nc.sbuf_tensor("mscr2", [128, 4096], F32))
        i8q = _e3(nc.sbuf_tensor("i8q", [128, 4 * FI], I8))
        s2r = _e3(nc.sbuf_tensor("s2r", [1, 640], F32))
        s2quad = _e3(nc.sbuf_tensor("s2quad", [128, 512], F32))
        qcol = _e3(nc.sbuf_tensor("qcol", [128, 8], F32))
        wdr = _e3(nc.sbuf_tensor("wdr", [128, 8 * 512], BF16))
        xsl = _e3(nc.sbuf_tensor("xsl", [128, 4 * 512], F32))
        ot0 = _e3(nc.sbuf_tensor("ot0", [128, 512], F32))
        ot1 = _e3(nc.sbuf_tensor("ot1", [128, 512], F32))
        ps_h0 = _e3(nc.psum_tensor("ps_h0", [128, 512], F32))
        ps_h1 = _e3(nc.psum_tensor("ps_h1", [128, 512], F32))
        ps_t = _e3(nc.psum_tensor("ps_t", [128, 512], F32))
        ps_d0 = _e3(nc.psum_tensor("ps_d0", [128, 512], F32))
        ps_d1 = _e3(nc.psum_tensor("ps_d1", [128, 512], F32))
        ps_d2 = _e3(nc.psum_tensor("ps_d2", [128, 512], F32))
        ps_d3 = _e3(nc.psum_tensor("ps_d3", [128, 512], F32))
        ps_h = [ps_h0, ps_h1]
        ps_d = [ps_d0, ps_d1, ps_d2, ps_d3]
        ots = [ot0, ot1]
        st = {"interb": None, "zq_ev": [None, None], "i8": {}, "pst": None,
              "n2done": {}, "p4mm_last": None, "p4_out": {}, "ring": [None] * 8,
              "ring_use": [None] * 8, "x": [None] * 4}

        def p3_load_bf(blk):
            toks = []
            blk_dbg = blk
            for piece in range(4):
                toks.append(s.dma(
                    c.ch_in[blk % 2],
                    interb[:, piece * 2048:(piece + 1) * 2048].rearrange(
                        "p (cc t) -> p cc t", t=128),
                    c.inter_d[piece * 2048:(piece + 1) * 2048,
                              blk * 128:(blk + 1) * 128].rearrange(
                        "(cc p) t -> p cc t", p=128),
                    deps=([st["interb"]] if st["interb"] else [])
                    + ([st.get("interb_rd")] if st.get("interb_rd") else [])))
            tprev = toks
            for stg in range(6):
                h = 32 >> stg
                va = interb[:, :].rearrange("p (bb two c t) -> p bb two (c t)",
                                            two=2, c=h, t=128)
                a_view = va[:, :, 0, :]
                b_view = va[:, :, 1, :]
                t1 = s.op("vector", lambda e, a=a_view, b=b_view: e.tensor_tensor(
                    a, a, b, ALU.add), deps=tprev)
                t2 = s.op("vector", lambda e, a=a_view, b=b_view: e.scalar_tensor_tensor(
                    b, b, -2.0, a, ALU.mult, ALU.add), deps=[t1])
                tprev = [t2]
            st["interb"] = tprev[0]
            if c.debug and blk == 0 and not getattr(c, "dbg_done", False):
                st["interb"] = s.dma(c.ch_misc, c.dbg["bf"][:, :], interb[:, :],
                                     deps=tprev)
            return tprev[0]

        def p3_post(blk, bt_done):
            ev_toks = []
            for g in range(16):
                pb = ps_h[g % 2]
                prev = st["zq_ev"][g % 2]
                tm = s.op("tensor", lambda e, g=g, pb=pb: e.matmul(
                    pb[:, :], c.h128b[:], interb[:, g * 512:(g + 1) * 512],
                    start=True, stop=True),
                    deps=[bt_done, c.tok_h] + ([prev] if prev else []))
                st["interb_rd"] = tm
                tev = s.op("vector", lambda e, g=g, pb=pb: e.tensor_copy(
                    zq[:, g * 512:(g + 1) * 512], pb[:, :]), deps=[tm])
                st["zq_ev"][g % 2] = tev
                ev_toks.append(tev)
            if c.debug and blk == 0 and not getattr(c, "dbg_done", False):
                ev_toks = ev_toks + [s.dma(c.ch_misc, c.dbg["zq"][:, :], zq[:, :],
                                           deps=ev_toks)]
            tm1 = s.op("vector", lambda e: e.tensor_tensor(
                mscr[:, :], zq[:, 0:4096], zq[:, 4096:8192], ALU.max), deps=ev_toks,
                sync_out=True)
            tm1b = s.op("vector", lambda e: e.tensor_tensor(
                mscr2[:, :], zq[:, 0:4096], zq[:, 4096:8192], ALU.min), deps=ev_toks,
                sync_out=True)
            tprev, tprevb = tm1, tm1b
            w = 2048
            while w >= 128:
                tprev = s.op("vector", lambda e, w=w: e.tensor_tensor(
                    mscr[:, 0:w], mscr[:, 0:w], mscr[:, w:2 * w], ALU.max),
                    deps=[tprev], sync_out=True)
                tprevb = s.op("vector", lambda e, w=w: e.tensor_tensor(
                    mscr2[:, 0:w], mscr2[:, 0:w], mscr2[:, w:2 * w], ALU.min),
                    deps=[tprevb], sync_out=True)
                w //= 2
            tt = s.op("tensor", lambda e: e.transpose(ps_t[:, 0:128], mscr[:, 0:128],
                                                      c.id_f[:]),
                      deps=[tprev, c.tok_idf] + ([st["pst"]] if st["pst"] else []))
            tmxa = s.op("vector", lambda e: e.reduce_max(qcol[:, 4:5], ps_t[:, 0:128], AX),
                        deps=[tt], sync_out=True)
            ttb = s.op("tensor", lambda e: e.transpose(ps_t[:, 128:256], mscr2[:, 0:128],
                                                       c.id_f[:]), deps=[tprevb, tt, tmxa])
            tmxb = s.op("vector", lambda e: e.tensor_reduce(
                qcol[:, 5:6], ps_t[:, 128:256], AX, ALU.min), deps=[ttb], sync_out=True)
            tmxc = s.op("vector", lambda e: e.tensor_scalar(
                qcol[:, 6:7], qcol[:, 5:6], -1.0, None, ALU.mult), deps=[tmxb], sync_out=True)
            tmx = s.op("vector", lambda e: e.tensor_max(
                qcol[:, 0:1], qcol[:, 4:5], qcol[:, 6:7]), deps=[tmxa, tmxc], sync_out=True)
            tc1 = s.op("vector", lambda e: e.tensor_scalar(
                qcol[:, 1:2], qcol[:, 0:1], C8192, Q_EPS, ALU.mult, ALU.max), deps=[tmx], sync_out=True)
            tc2 = s.op("vector", lambda e: e.reciprocal(qcol[:, 2:3], qcol[:, 1:2]),
                       deps=[tc1], sync_out=True)
            tc3 = s.op("vector", lambda e: e.tensor_scalar(
                qcol[:, 3:4], qcol[:, 2:3], 128.0 * C8192, None, ALU.mult), deps=[tc2], sync_out=True)
            ta2 = s.op("vector", lambda e, blk=blk: e.tensor_scalar(
                c.a2cols[:, blk:blk + 1], qcol[:, 1:2], 1.0 / 128.0, None, ALU.mult),
                deps=[tc1], sync_out=True)
            tbeta = s.op("vector", lambda e, blk=blk, idn=idn: e.tensor_scalar(
                c.betacols[:, blk:blk + 1], c.a2cols[:, blk:blk + 1],
                c.cols[:, 4 * idn + 2:4 * idn + 3], 0.5, ALU.mult, ALU.mult),
                deps=[ta2], sync_out=True)
            tt2 = s.op("tensor", lambda e: e.transpose(ps_t[0:1, 256:384], qcol[:, 3:4],
                                                       c.id_f[:]),
                       deps=[tc3, c.tok_idf, tmx, tmxb])
            st["pst"] = tt2
            tr1 = s.op("vector", lambda e: e.tensor_copy(s2r[0:1, 0:128],
                                                         ps_t[0:1, 256:384]), deps=[tt2], sync_out=True)
            tq4 = []
            for q4 in range(4):
                tq4.append(s.op("vector", lambda e, q4=q4: e.tensor_copy(
                    s2r[0:1, 128 + q4 * 128:256 + q4 * 128], s2r[0:1, 0:128]),
                    deps=[tr1]))
            tmb = s.op("tensor", lambda e: e.matmul(ps_t[:, 0:512], c.ones1[:, :],
                                                    s2r[0:1, 128:640], start=True,
                                                    stop=True),
                       deps=tq4 + [c.tok_ones, tr1, tmx])
            st["pst"] = tmb
            tbc = s.op("vector", lambda e: e.tensor_copy(s2quad[:, :], ps_t[:, 0:512]),
                       deps=[tmb])
            st["pst"] = tbc
            big_toks = []
            for r8 in range(8):
                big_toks.append(s.op("vector", lambda e, r8=r8: e.tensor_copy(
                    mscr[:, r8 * 512:(r8 + 1) * 512], s2quad[:, :]), deps=[tbc]))
            hh = []
            for hf in range(2):
                hh.append(s.op("vector", lambda e, hf=hf: e.tensor_tensor(
                    zq[:, hf * 4096:(hf + 1) * 4096], zq[:, hf * 4096:(hf + 1) * 4096],
                    mscr[:, :], ALU.mult), deps=big_toks))
            if c.debug and blk == 0 and not getattr(c, "dbg_done", False):
                s.dma(c.ch_misc, c.dbg["qc"][:, :], qcol[:, :], deps=hh + [tbeta])
            ti8 = s.op("vector", lambda e, blk=blk: e.tensor_copy(
                i8q[:, (blk % 4) * FI:(blk % 4 + 1) * FI], zq[:, :]),
                deps=hh + [st["i8"].get(blk % 4)] +
                ([Tok(c.ch_misc, s.chan_count[c.ch_misc])] if (c.debug and blk == 0) else []))
            st["i8"][blk % 4] = ti8
            return ti8

        def p3_n2copy(blk, ti8, extra):
            qi = blk % 4
            srcv = i8q[:, qi * FI:(qi + 1) * FI].rearrange("p (cc t) -> p cc t", t=128)
            dstv = n2sb[:, :].rearrange("p (cc t) -> p cc t", t=512)[
                :, :, qi * 128:(qi + 1) * 128]
            t = s.op("vector", lambda e, srcv=srcv, dstv=dstv: e.tensor_copy(dstv, srcv),
                     deps=[ti8] + extra)
            st["n2done"][blk] = t
            return t

        def p4_ob(q, ob, n2toks):
            mm_last = [None] * 4
            xdma = [None] * 4
            for tc in range(4):
                xdma[tc] = s.dma(c.ch_in[tc % 2], xsl[:, tc * 512:(tc + 1) * 512],
                                 in_ap[(q * 4 + tc) * 128:(q * 4 + tc + 1) * 128,
                                       ob * 512:(ob + 1) * 512],
                                 deps=[st["x"][tc]] if st["x"][tc] else [])
            for f in range(64):
                slot = f % 8
                st["ring"][slot] = s.dma(
                    c.ch_w[f % 2], wdr[:, slot * 512:(slot + 1) * 512],
                    c.tq["down"][ob * 512:(ob + 1) * 512, f * 128:(f + 1) * 128],
                    deps=([st["ring_use"][slot]] if st["ring_use"][slot] else [])
                    + list(c.tern_tok["down"]), transpose=True)
                for tc in range(4):
                    mm_last[tc] = s.op("tensor", lambda e, f=f, tc=tc, slot=slot: e.matmul(
                        ps_d[tc][:, :],
                        n2sb[:, f * 512 + tc * 128:f * 512 + (tc + 1) * 128],
                        wdr[:, slot * 512:(slot + 1) * 512],
                        start=(f == 0), stop=(f == 63)),
                        deps=[st["ring"][slot]] + (n2toks if f == 0 else [])
                        + ([st["p4_out"].get(("war", tc))] if (f == 0 and
                           st["p4_out"].get(("war", tc))) else []))
                st["ring_use"][slot] = mm_last[3]
            st["p4mm_last"] = mm_last[3]
            return mm_last, xdma

        def p4_ob_evac(q, ob, mm_last, xdma):
            for tc in range(4):
                par = tc % 2
                tev = s.op("vector", lambda e, tc=tc, q=q, par=par: e.scalar_tensor_tensor(
                    ots[par][:], ps_d[tc][:, :],
                    c.betacols[:, q * 4 + tc:q * 4 + tc + 1],
                    xsl[:, tc * 512:(tc + 1) * 512], ALU.mult, ALU.add),
                    deps=[mm_last[tc], xdma[tc]]
                    + ([st["p4_out"].get(("dma", par))] if st["p4_out"].get(("dma", par)) else []))
                tst = s.dma(c.ch_out[par],
                            out_ap[(q * 4 + tc) * 128:(q * 4 + tc + 1) * 128,
                                   ob * 512:(ob + 1) * 512], ots[par][:], deps=[tev])
                st["x"][tc] = tev
                st["p4_out"][("war", tc)] = tev
                st["p4_out"][("dma", par)] = tst

        for bi in range(4):
            bt = p3_load_bf(bi)
            ti8 = p3_post(bi, bt)
            p3_n2copy(bi, ti8, [])
        for q in range(4):
            n2toks = [st["n2done"][q * 4 + bi] for bi in range(4)]
            if c.debug and q == 0 and not getattr(c, "dbg_done", False):
                s.dma(c.ch_misc, c.dbg["n2"][:, :], n2sb[:, :], deps=n2toks)
                s.dma(c.ch_misc, c.dbg["beta"][:, :], c.betacols[:, :], deps=n2toks)
                n2toks = n2toks + [Tok(c.ch_misc, s.chan_count[c.ch_misc])]
            for ob in range(4):
                mm_last, xdma = p4_ob(q, ob, n2toks if ob == 0 else [])
                if q + 1 < 4:
                    blk = (q + 1) * 4 + ob
                    bt = p3_load_bf(blk)
                    p4_ob_evac(q, ob, mm_last, xdma)
                    p3_post(blk, bt)
                else:
                    p4_ob_evac(q, ob, mm_last, xdma)
            if q + 1 < 4:
                gatep4 = [Tok("tensor", s.counts["tensor"])]
                for bi in range(4):
                    blk = (q + 1) * 4 + bi
                    p3_n2copy(blk, st["i8"][blk % 4], gatep4)



def attn_sublayer(c, in_ap, out_ap):
    s, nc = c.s, _proxy(c)
    iq = c.mat_idx["qkv"]
    import contextlib as _cl
    with _cl.ExitStack() as _st4:
        _e4 = _st4.enter_context
        gqkv_bc = _e4(nc.sbuf_tensor("gqkv_bc", [128, D], F32))
        gn_bc = _e4(nc.sbuf_tensor("gn_bc", [128, D], F32))
        gp_bc = _e4(nc.sbuf_tensor("gp_bc", [128, D], F32))
        bc_all = []
        tprev = None
        with nc.sbuf_tensor("grow", [1, D], F32) as grow:
            for vec, dst in [(c.g_qkv, gqkv_bc), (c.g_norm, gn_bc), (c.g_proj, gp_bc)]:
                tl = s.dma(c.ch_misc, grow[:, :], vec[:].rearrange("(o d) -> o d", o=1),
                           deps=[tprev] if tprev else [])
                toks = emit_bc_rows(c, grow, dst, None, [tl])
                tprev = toks[-1]
                bc_all += toks
        s.set_gate(s.frontier())

        with nc.sbuf_tensor("hqF", [128, NT * D], BF16) as hqF:
            import contextlib as _cl
            with _cl.ExitStack() as _st5:
                _e5 = _st5.enter_context
                xt0 = _e5(nc.sbuf_tensor("xta", [128, D], F32))
                xt1 = _e5(nc.sbuf_tensor("xtb", [128, D], F32))
                y2t = _e5(nc.sbuf_tensor("y2", [128, D], F32))
                hqt = _e5(nc.sbuf_tensor("hq", [128, D], F32))
                hqbt = _e5(nc.sbuf_tensor("hqb", [128, D], BF16))
                dmp = _e5(nc.sbuf_tensor("dmpa", [128, D], F32))
                lncol = _e5(nc.sbuf_tensor("lncola", [128, 16], F32))
                ps_tr0 = _e5(nc.psum_tensor("ps_tra0", [128, 512], BF16))
                ps_tr1 = _e5(nc.psum_tensor("ps_tra1", [128, 512], BF16))
                xts = [xt0, xt1]
                prev_rd = [None, None]
                prev_rd2 = [None, None]
                prev_ev = [None, None]
                prev_q = [None]
                for i in range(NT):
                    b = i % 2
                    tl = s.dma(c.ch_in[b], xts[b][:], in_ap[i * 128:(i + 1) * 128, :],
                               deps=[prev_rd[b], prev_rd2[b]] if prev_rd[b] else [])
                    ty, readers = emit_ln(c, xts[b][:], y2t[:], dmp[:], lncol, [tl],
                                          war=[prev_q[0]] if prev_q[0] else [])
                    prev_rd[b] = ty
                    prev_rd2[b] = readers[0]
                    t1 = s.op("scalar", lambda e: e.activation(
                        dmp[:], y2t[:], AF.Square, accum_out=lncol[:, 8:9]), deps=[ty])
                    t2 = s.op("vector", lambda e: e.tensor_scalar(
                        lncol[:, 9:10], lncol[:, 8:9], 1.0 / D, EPS_A, ALU.mult, ALU.add),
                        deps=[t1], sync_out=True)
                    t3 = s.op("scalar", lambda e: e.activation(
                        lncol[:, 10:11], lncol[:, 9:10], AF.Sqrt), deps=[t2])
                    t4 = s.op("vector", lambda e: e.reciprocal(
                        lncol[:, 11:12], lncol[:, 10:11]), deps=[t3], sync_out=True)
                    t5 = s.op("vector", lambda e: e.scalar_tensor_tensor(
                        hqt[:], y2t[:], lncol[:, 11:12], gqkv_bc[:, :],
                        ALU.mult, ALU.mult), deps=[t4] + bc_all)
                    prev_q[0] = t5
                    t6 = s.op("vector", lambda e: e.tensor_copy(hqbt[:], hqt[:]),
                              deps=[t5] + [t for t in prev_ev if t])
                    emit_transpose_quad(
                        c, hqbt,
                        lambda g, i=i: hqF[:, :].rearrange("p (j t) -> p j t", t=D)[
                        :, 4 * g:4 * g + 4, i * 128:(i + 1) * 128],
                        [ps_tr0, ps_tr1], prev_ev, t6, f"hqF{i}")
                a1_done = s.frontier()
                if c.debug:
                    s.dma(c.ch_misc, c.dbg["hqF"][:, :], hqF[:, :], deps=a1_done)
            s.set_gate(a1_done)

            import contextlib as _cl
            with _cl.ExitStack() as _st6:
                _e6 = _st6.enter_context
                wq0 = _e6(nc.sbuf_tensor("wq0", [128, 16 * 512], BF16))
                wq1 = _e6(nc.sbuf_tensor("wq1", [128, 16 * 512], BF16))
                eva0 = _e6(nc.sbuf_tensor("eva0", [128, 512], BF16))
                eva1 = _e6(nc.sbuf_tensor("eva1", [128, 512], BF16))
                ps_q0 = _e6(nc.psum_tensor("ps_q0", [128, 512], F32))
                ps_q1 = _e6(nc.psum_tensor("ps_q1", [128, 512], F32))
                wqs, evs, psq = [wq0, wq1], [eva0, eva1], [ps_q0, ps_q1]
                slab_tok = [None, None]
                last_mm = [None, None]
                prev_ev2 = [None, None]

                def load_qgroup(og, b, deps):
                    toks = []
                    for f in range(16):
                        toks.append(s.dma(c.ch_w[f % 2],
                                          wqs[b][:, f * 512:(f + 1) * 512],
                                          c.tq["qkv"][og * 512:(og + 1) * 512,
                                                      f * 128:(f + 1) * 128],
                                          deps=deps, transpose=True))
                    return toks

                slab_tok[0] = load_qgroup(0, 0, list(c.tern_tok["qkv"]))
                for og in range(12):
                    b = og % 2
                    if og + 1 < 12:
                        nb = (og + 1) % 2
                        slab_tok[nb] = load_qgroup(
                            og + 1, nb,
                            ([last_mm[nb]] if last_mm[nb] else []) + list(c.tern_tok["qkv"]))
                    for oc4 in range(4):
                        oc = og * 4 + oc4
                        ofs = oc4 * 128
                        for tg in range(4):
                            par = (oc4 * 4 + tg) % 2
                            depf = slab_tok[b] + ([prev_ev2[par]] if prev_ev2[par] else [])
                            mm = None
                            for f in range(16):
                                mm = s.op("tensor", lambda e, f=f, b=b, tg=tg,
                                          ofs=ofs, par=par: e.matmul(
                                    psq[par][:, :],
                                    wqs[b][:, f * 512 + ofs:f * 512 + ofs + 128],
                                    hqF[:, f * D + tg * 512:f * D + (tg + 1) * 512],
                                    start=(f == 0), stop=(f == 15)),
                                    deps=(depf if f == 0 else []))
                            last_mm[b] = mm
                            tev = s.op("vector", lambda e, par=par, iq=iq: e.tensor_scalar(
                                evs[par][:], psq[par][:, :],
                                c.cols[:, 4 * iq + 2:4 * iq + 3], None, ALU.mult),
                                deps=[mm])
                            prev_ev2[par] = s.dma(
                                c.ch_out[par],
                                c.qkv_d[oc * 128:(oc + 1) * 128, tg * 512:(tg + 1) * 512],
                                evs[par][:], deps=[tev])
                a2_done = s.frontier()
                if c.debug:
                    s.dma(c.ch_misc, c.dbg["qkv0"][:, :],
                          c.qkv_d[:, 0:128], deps=a2_done)
        s.set_gate(a2_done)
        attn_scan_proj(c, in_ap, out_ap, gn_bc, gp_bc)


def attn_scan_proj(c, in_ap, out_ap, gn_bc, gp_bc):
    s, nc = c.s, _proxy(c)
    ip = c.mat_idx["proj"]
    import contextlib as _cl
    with _cl.ExitStack() as _st7:
        _e7 = _st7.enter_context
        z2F = _e7(nc.sbuf_tensor("z2F", [128, NT * D], BF16))
        S = _e7(nc.sbuf_tensor("S", [128, HEADS * 128], F32))
        Sb = _e7(nc.sbuf_tensor("Sb", [128, HEADS * 128], BF16))
        maskT = _e7(nc.sbuf_tensor("maskT", [128, HEADS * 128], F32))
        lamt_bc = _e7(nc.sbuf_tensor("lamt_bc", [128, HEADS * 128], BF16))
        lamr_bc = _e7(nc.sbuf_tensor("lamr_bc", [128, HEADS * 128], BF16))
        iotaG = _e7(nc.sbuf_tensor("iotaG", [128, 128], F32))
        indic = _e7(nc.sbuf_tensor("indic", [128, 128], F32))
        tG = s.op("gpsimd", lambda e: e.iota(iotaG[:], [[1, 128]], base=0,
                                             channel_multiplier=-1,
                                             allow_small_or_imprecise_dtypes=True))
        tGp = s.op("vector", lambda e: e.tensor_scalar(iotaG[:], iotaG[:], 0.0, None,
                                                       ALU.max), deps=[tG])
        tG2 = s.op("gpsimd", lambda e: e.iota(indic[:], [[1, 128]], base=0,
                                              channel_multiplier=-1,
                                              allow_small_or_imprecise_dtypes=True))
        tI = s.op("vector", lambda e: e.tensor_scalar(indic[:], indic[:], 0.0, None,
                                                      ALU.is_ge), deps=[tG2])
        mask_toks = []
        for h in range(HEADS):
            t1 = s.op("scalar", lambda e, h=h: e.activation(
                maskT[:, h * 128:(h + 1) * 128], iotaG[:], AF.Exp,
                scale=c.neg_s[:, h:h + 1]), deps=[tGp, c.tok_negs])
            t2 = s.op("vector", lambda e, h=h: e.tensor_mul(
                maskT[:, h * 128:(h + 1) * 128], maskT[:, h * 128:(h + 1) * 128],
                indic[:]), deps=[t1, tI])
            mask_toks.append(t2)
        tiorow = s.op("gpsimd", lambda e: e.iota(iotaG[:], [[1, 128]], base=0,
                                                 channel_multiplier=0,
                                                 allow_small_or_imprecise_dtypes=True),
                      deps=mask_toks)
        tiorev = s.op("gpsimd", lambda e: e.iota(indic[:], [[-1, 128]], base=128,
                                                 channel_multiplier=0,
                                                 allow_small_or_imprecise_dtypes=True),
                      deps=[tI] + mask_toks)
        lam_toks = []
        for h in range(HEADS):
            lam_toks.append(s.op("scalar", lambda e, h=h: e.activation(
                lamt_bc[:, h * 128:(h + 1) * 128], iotaG[:], AF.Exp,
                scale=c.neg_s[:, h:h + 1]), deps=[tiorow, c.tok_negs]))
            lam_toks.append(s.op("scalar", lambda e, h=h: e.activation(
                lamr_bc[:, h * 128:(h + 1) * 128], indic[:], AF.Exp,
                scale=c.neg_s[:, h:h + 1]), deps=[tiorev, c.tok_negs]))
        if c.debug:
            s.dma(c.ch_misc, c.dbg["mask"][:, :], maskT[:, :], deps=mask_toks)
            s.dma(c.ch_misc, c.dbg["lamt"][:, :], lamt_bc[:, :], deps=lam_toks)
            s.dma(c.ch_misc, c.dbg["lamr"][:, :], lamr_bc[:, :], deps=lam_toks)
            s.dma(c.ch_misc, c.dbg["lamT"][:, :], c.lamT[:, :], deps=lam_toks)
        tS0 = s.op("vector", lambda e: e.memset(S[:, :], 0.0))
        tSb = s.op("vector", lambda e: e.memset(Sb[:, :], 0.0), deps=[tS0])

        import contextlib
        with contextlib.ExitStack() as _st:
            _e = _st.enter_context
            slab0 = _e(nc.sbuf_tensor("slab0", [128, 48 * 128], BF16))
            slab1 = _e(nc.sbuf_tensor("slab1", [128, 48 * 128], BF16))
            qtb = _e(nc.sbuf_tensor("qtb", [128, HEADS * 128], BF16))
            ktb = _e(nc.sbuf_tensor("ktb", [128, 128], BF16))
            scm = _e(nc.sbuf_tensor("scm", [128, 128], BF16))
            ktT = _e(nc.sbuf_tensor("ktT", [128, 128], BF16))
            vT = _e(nc.sbuf_tensor("vT", [128, 128], BF16))
            outc = _e(nc.sbuf_tensor("outc", [128, D], F32))
            z1c = _e(nc.sbuf_tensor("z1c", [128, D], F32))
            z2b = _e(nc.sbuf_tensor("z2b", [128, D], BF16))
            dmp2 = _e(nc.sbuf_tensor("dmp2", [128, D], F32))
            ncol = _e(nc.sbuf_tensor("ncol", [128, 8], F32))
            ps_sc = _e(nc.psum_tensor("ps_sc", [128, 128], F32))
            ps_tt = _e(nc.psum_tensor("ps_tt", [128, 256], BF16))
            ps_tt2 = _e(nc.psum_tensor("ps_tt2", [128, 256], BF16))
            ps_o = _e(nc.psum_tensor("ps_o", [128, 128], F32))
            ps_kv = _e(nc.psum_tensor("ps_kv", [128, 128], F32))
            ps_trz0 = _e(nc.psum_tensor("ps_trz0", [128, 512], BF16))
            ps_trz1 = _e(nc.psum_tensor("ps_trz1", [128, 512], BF16))
            slabs = [slab0, slab1]
            prev_slab_rd = [None, None]
            prev_zev = [None, None]
            pt = {"t2": None, "t12": None, "t11": None, "t67": None, "t10": None}
            for i in range(NCH):
                b = i % 2
                tld = []
                for piece in range(4):
                    tld.append(s.dma(
                        c.ch_in[b],
                        slabs[b][:, piece * 1536:(piece + 1) * 1536].rearrange(
                            "p (a t) -> p a t", t=128),
                        c.qkv_d[piece * 1536:(piece + 1) * 1536,
                                i * 128:(i + 1) * 128].rearrange(
                            "(a p) t -> p a t", p=128),
                        deps=[prev_slab_rd[b]] if prev_slab_rd[b] else []))
                tcast = tld[0]
                tqt = s.op("vector", lambda e, b=b: e.tensor_tensor(
                    qtb[:, :], slabs[b][:, 0:HEADS * 128], lamt_bc[:, :], ALU.mult),
                    deps=tld + lam_toks + ([pt["t11"]] if pt["t11"] else []))
                last_k_rd = None
                for h in range(HEADS):
                    qs = slabs[b][:, h * 128:(h + 1) * 128]
                    ks = slabs[b][:, (16 + h) * 128:(17 + h) * 128]
                    vs = slabs[b][:, (32 + h) * 128:(33 + h) * 128]
                    t1 = s.op("tensor", lambda e, ks=ks, qs=qs: e.matmul(
                        ps_sc[:, :], ks, qs, start=True, stop=True),
                        deps=tld + ([pt["t2"]] if pt["t2"] else []))
                    t2 = s.op("vector", lambda e, h=h: e.tensor_mul(
                        scm[:], ps_sc[:, :], maskT[:, h * 128:(h + 1) * 128]),
                        deps=[t1] + mask_toks)
                    pt["t2"] = t2
                    t3 = s.op("vector", lambda e, h=h, b=b: e.tensor_tensor(
                        ktb[:], slabs[b][:, (16 + h) * 128:(17 + h) * 128],
                        lamr_bc[:, h * 128:(h + 1) * 128], ALU.mult),
                        deps=tld + lam_toks)
                    last_k_rd = t3
                    t4 = s.op("tensor", lambda e: e.transpose(ps_tt[:, 0:128], ktb[:],
                                                              c.id_b[:]),
                              deps=[t3, c.tok_id] + ([pt["t67"]] if pt["t67"] else []))
                    t5 = s.op("tensor", lambda e, vs=vs: e.transpose(
                        ps_tt2[:, 0:128], vs, c.id_b[:]),
                        deps=tld + [t4] + ([pt["t67"]] if pt["t67"] else []))
                    t6 = s.op("vector", lambda e: e.tensor_copy(ktT[:], ps_tt[:, 0:128]),
                              deps=[t4])
                    t7 = s.op("vector", lambda e: e.tensor_copy(vT[:], ps_tt2[:, 0:128]),
                              deps=[t5])
                    pt["t67"] = t7
                    if c.debug and i == 0 and h == 0:
                        s.dma(c.ch_misc, c.dbg["scm"][:, :], scm[:], deps=[t2])
                        s.dma(c.ch_misc, c.dbg["vT"][:, :], vT[:], deps=[t7])
                        s.dma(c.ch_misc, c.dbg["slab"][:, :], slabs[b][:, :], deps=tld)
                        t2 = Tok(c.ch_misc, s.chan_count[c.ch_misc])
                    t8 = s.op("tensor", lambda e: e.matmul(ps_o[:, :], scm[:], vT[:],
                                                           start=True, stop=False),
                              deps=[t2, t7] + ([pt["t12"]] if pt["t12"] else []))
                    t9 = s.op("tensor", lambda e, h=h: e.matmul(
                        ps_o[:, :], qtb[:, h * 128:(h + 1) * 128],
                        Sb[:, h * 128:(h + 1) * 128], start=False, stop=True),
                        deps=[tqt, t8, tSb])
                    t10 = s.op("tensor", lambda e: e.matmul(ps_kv[:, :], ktT[:], vT[:],
                                                            start=True, stop=True),
                               deps=[t6, t7, t9] + ([pt["t11"]] if pt["t11"] else []))
                    t11 = s.op("vector", lambda e, h=h: e.scalar_tensor_tensor(
                        S[:, h * 128:(h + 1) * 128], S[:, h * 128:(h + 1) * 128],
                        c.lamT[:, h:h + 1], ps_kv[:, :], ALU.mult, ALU.add),
                        deps=[t10, c.tok_lamT, tSb], sync_out=True)
                    pt["t11"] = t11
                    t12 = s.op("scalar", lambda e, h=h: e.activation(
                        outc[:, h * 128:(h + 1) * 128], ps_o[:, :], AF.Identity,
                        scale=SCALE_ATT),
                        deps=[t9] + ([pt.get("tz1")] if (h == 0 and pt.get("tz1")) else [])
                        + ([pt.get("outc_guard")] if (h == 0 and pt.get("outc_guard")) else []))
                    pt["t12"] = t12
                    if c.debug and i == 0 and h == 0:
                        tSb_tap = s.dma(c.ch_misc, c.dbg["Sb"][:, :], Sb[:, :], deps=[tSb])
                        s.dma(c.ch_misc, c.dbg["o0"][:, :], outc[:, 0:128], deps=[t12])
                        s.dma(c.ch_misc, c.dbg["qtb"][:, :], qtb[:, :], deps=[tqt])
                        pt["t12"] = Tok(c.ch_misc, s.chan_count[c.ch_misc])
                prev_slab_rd[b] = Tok("tensor", s.counts["tensor"])
                tSb = s.op("vector", lambda e: e.tensor_copy(Sb[:, :], S[:, :]),
                           deps=[pt["t11"], pt["t12"], pt["t10"] or pt["t11"]])
                if c.debug and i == 0:
                    tdo = s.dma(c.ch_misc, c.dbg["outc"][:, :], outc[:, :],
                                deps=[pt["t12"], pt["t11"]])
                    pt["outc_guard"] = tdo
                tn1 = s.op("scalar", lambda e: e.activation(
                    dmp2[:], outc[:], AF.Square, accum_out=ncol[:, 0:1]),
                    deps=[pt["t12"], pt["t11"]])
                tn2 = s.op("vector", lambda e: e.tensor_scalar(
                    ncol[:, 1:2], ncol[:, 0:1], 1.0 / D, 1e-6, ALU.mult, ALU.add),
                    deps=[tn1], sync_out=True)
                tn3 = s.op("scalar", lambda e: e.activation(ncol[:, 2:3], ncol[:, 1:2],
                                                            AF.Sqrt), deps=[tn2])
                tn4 = s.op("vector", lambda e: e.reciprocal(ncol[:, 3:4], ncol[:, 2:3]),
                           deps=[tn3], sync_out=True)
                tz1 = s.op("vector", lambda e: e.scalar_tensor_tensor(
                    z1c[:], outc[:], ncol[:, 3:4], gn_bc[:, :], ALU.mult, ALU.mult),
                    deps=[tn4])
                pt["tz1"] = tz1
                tm1 = s.op("scalar", lambda e: e.activation(
                    dmp2[:], z1c[:], AF.Square, accum_out=ncol[:, 4:5]), deps=[tz1])
                tm2 = s.op("vector", lambda e: e.tensor_scalar(
                    ncol[:, 5:6], ncol[:, 4:5], 1.0 / D, EPS_A, ALU.mult, ALU.add),
                    deps=[tm1], sync_out=True)
                tm3 = s.op("scalar", lambda e: e.activation(ncol[:, 6:7], ncol[:, 5:6],
                                                            AF.Sqrt), deps=[tm2])
                tm4 = s.op("vector", lambda e: e.reciprocal(ncol[:, 7:8], ncol[:, 6:7]),
                           deps=[tm3], sync_out=True)
                tz2 = s.op("vector", lambda e: e.scalar_tensor_tensor(
                    z1c[:], z1c[:], ncol[:, 7:8], gp_bc[:, :], ALU.mult, ALU.mult),
                    deps=[tm4])
                tzb = s.op("vector", lambda e: e.tensor_copy(z2b[:], z1c[:]),
                           deps=[tz2] + [t for t in prev_zev if t])
                emit_transpose_quad(
                    c, z2b,
                    lambda g, i=i: z2F[:, :].rearrange("p (j t) -> p j t", t=D)[
                        :, 4 * g:4 * g + 4, i * 128:(i + 1) * 128],
                    [ps_trz0, ps_trz1], prev_zev, tzb, f"z2F{i}")
            a3_done = s.frontier()
            if c.debug:
                s.dma(c.ch_misc, c.dbg["z2F"][:, :], z2F[:, :], deps=a3_done)
        s.set_gate(a3_done)

        import contextlib as _cl
        with _cl.ExitStack() as _st8:
            _e8 = _st8.enter_context
            wp = _e8(nc.sbuf_tensor("wp", [128, 64 * 512], BF16))
            xsl2 = _e8(nc.sbuf_tensor("xsl2", [128, 2 * 512], F32))
            oa0 = _e8(nc.sbuf_tensor("oa0", [128, 512], F32))
            oa1 = _e8(nc.sbuf_tensor("oa1", [128, 512], F32))
            ps_p0 = _e8(nc.psum_tensor("ps_p0", [128, 512], F32))
            ps_p1 = _e8(nc.psum_tensor("ps_p1", [128, 512], F32))
            wp_toks = []
            for ob in range(4):
                for f in range(16):
                    wp_toks.append(s.dma(
                        c.ch_w[f % 2],
                        wp[:, (ob * 16 + f) * 512:(ob * 16 + f + 1) * 512],
                        c.tq["proj"][ob * 512:(ob + 1) * 512, f * 128:(f + 1) * 128],
                        deps=list(c.tern_tok["proj"]), transpose=True))
            psp, oas = [ps_p0, ps_p1], [oa0, oa1]
            prev_dma = [None, None]
            prev_ev = [None, None]
            prev_x = [None, None]
            for tc in range(NT):
                for ob in range(4):
                    par = (tc * 4 + ob) % 2
                    tx = s.dma(c.ch_in[par], xsl2[:, par * 512:(par + 1) * 512],
                               in_ap[tc * 128:(tc + 1) * 128, ob * 512:(ob + 1) * 512],
                               deps=[prev_x[par]] if prev_x[par] else [])
                    mm = None
                    for f in range(16):
                        mm = s.op("tensor", lambda e, f=f, tc=tc, ob=ob, par=par: e.matmul(
                            psp[par][:, :],
                            z2F[:, f * D + tc * 128:f * D + (tc + 1) * 128],
                            wp[:, (ob * 16 + f) * 512:(ob * 16 + f + 1) * 512],
                            start=(f == 0), stop=(f == 15)),
                            deps=((wp_toks + ([prev_ev[par]] if prev_ev[par] else []))
                                  if f == 0 else []))
                    tev = s.op("vector", lambda e, par=par, ip=ip: e.scalar_tensor_tensor(
                        oas[par][:], psp[par][:, :], c.cols[:, 4 * ip + 2:4 * ip + 3],
                        xsl2[:, par * 512:(par + 1) * 512], ALU.mult, ALU.add),
                        deps=[mm, tx] + ([prev_dma[par]] if prev_dma[par] else []))
                    tst = s.dma(c.ch_out[par],
                                out_ap[tc * 128:(tc + 1) * 128, ob * 512:(ob + 1) * 512],
                                oas[par][:], deps=[tev])
                    prev_ev[par] = tev
                    prev_dma[par] = tst
                    prev_x[par] = tev



def kernel(L0_attn_x, L0_attn_qkv_w, L0_attn_qkv_norm_w, L0_attn_proj_w,
           L0_attn_proj_norm_w, L0_attn_norm_w, L0_attn_s,
           L0_ffn_w_gate, L0_ffn_w_up, L0_ffn_w_down):
    x = np.ascontiguousarray(np.asarray(L0_attn_x, dtype=np.float32))
    base = {
        "w_qkv": np.ascontiguousarray(np.asarray(L0_attn_qkv_w, np.float32)),
        "w_proj": np.ascontiguousarray(np.asarray(L0_attn_proj_w, np.float32)),
        "w_gate": np.ascontiguousarray(np.asarray(L0_ffn_w_gate, np.float32)),
        "w_up": np.ascontiguousarray(np.asarray(L0_ffn_w_up, np.float32)),
        "w_down": np.ascontiguousarray(np.asarray(L0_ffn_w_down, np.float32)),
        "g_qkv": np.ascontiguousarray(np.asarray(L0_attn_qkv_norm_w, np.float32)),
        "g_proj": np.ascontiguousarray(np.asarray(L0_attn_proj_norm_w, np.float32)),
        "g_norm": np.ascontiguousarray(np.asarray(L0_attn_norm_w, np.float32)),
        "s_att": np.ascontiguousarray(np.asarray(L0_attn_s, np.float32)),
    }
    base.update(host_constants())
    nc = build_program()
    in_maps = []
    for b in range(8):
        m = dict(base)
        m["x_in"] = x[b]
        in_maps.append(m)
    res = run_bass_kernel_spmd(nc, in_maps, core_ids=list(range(8)))
    out = np.stack([np.asarray(r["x_out"]) for r in res.results], axis=0)
    return out.astype(np.float32)
